# version 37
# baseline (speedup 1.0000x reference)
import os
import numpy as np
import ml_dtypes
import concourse.bass as bass
import concourse.mybir as mybir
from concourse.bass_utils import run_bass_kernel_spmd

F32 = mybir.dt.float32
BF16 = mybir.dt.bfloat16
I8 = mybir.dt.int8
AF = mybir.ActivationFunctionType
ALU = mybir.AluOpType
AX = mybir.AxisListType

M32 = 12582912.0
M52 = 6755399441055744.0

NCORE = 8
BPC = 2
S, D, H, HD, HID = 2048, 1024, 16, 64, 256
T = S // 128

SIM_FP64 = os.environ.get("BASS_SIM_FP64") == "1"


class Buf:

    def __init__(self, t, W, off, w):
        self.t, self.W, self.off, self.w = t, W, off, w

    def ap(self, c0=0, cw=None, p0=0, pn=128):
        cw = self.w - c0 if cw is None else cw
        return bass.AP(self.t, p0 * self.W + self.off + c0, [[self.W, pn], [1, cw]])

    def ap3(self, c0, pairs, p0=0, pn=128):
        return bass.AP(self.t, p0 * self.W + self.off + c0,
                       [[self.W, pn]] + [[st, c] for (st, c) in pairs])


class Arena:
    def __init__(self, t, W):
        self.t, self.W, self.off = t, W, 0

    def alloc(self, w):
        assert self.off + w <= self.W, f"arena overflow: {self.off}+{w} > {self.W}"
        b = Buf(self.t, self.W, self.off, w)
        self.off += w
        return b

    def overlay(self, at, w, extra=0):
        assert at.off + extra + w <= self.W
        return Buf(self.t, self.W, at.off + extra, w)


DEBUG = False


class MK:
    def __init__(self, debug=False):
        self.nc = bass.Bass("TRN2", target_bir_lowering=False)
        self.cmds = {k: [] for k in "QPTVS"}
        self.meta = {k: [] for k in "QPTVS"}
        self.n = {}
        self.sems = {}
        self.n_instr = 0
        self.attn_tok = 0
        self.bt_free = 0
        self.bt_free_h = [0, 0]
        self.bt_half = 0
        self.l_gelu_tok = 0
        self.n_at = 0
        self.mat_tok = 0
        self.dh_tok = 0
        self.b_red_tok = 0
        self.h_red_tok = 0
        self.keysT_ready = 0
        self.debug = debug or DEBUG
        self.dump_names = []

    def dump(self, name, src_ap, shape):
        if not self.debug:
            return
        dt = nc_dt = F32
        dout = self.nc.dram_tensor("dbg_" + name, list(shape), nc_dt, kind="ExternalOutput")
        self.dump_names.append("dbg_" + name)
        self.wait("P", "cV", self.n["cV"])
        self.wait("P", "cS", self.n["cS"])
        self.wait("P", "cT", self.n["cT"])
        tok = self.dma("P", lambda e: e.dma_start(
            out=bass.AP(dout, 0, [[shape[1], shape[0]], [1, shape[1]]]), in_=src_ap), "sDB")
        for eng in "TVSQ":
            self.wait(eng, "sDB", tok)

    def wait(self, eng, sem, val):
        if val <= 0:
            return
        h = self.sems[sem]
        self.cmds[eng].append(lambda e, h=h, v=val: e.wait_ge(h, v))
        self.meta[eng].append(("w", sem, val, getattr(self, "ctx", "?")))

    def op(self, eng, fn, inc=None, step=1, drain=False):
        tok = None
        if inc is not None:
            self.n[inc] += step
            tok = self.n[inc]
            h = self.sems[inc]
            self.cmds[eng].append(lambda e, fn=fn, h=h, st=step: fn(e).then_inc(h, st))
        else:
            self.cmds[eng].append(lambda e, fn=fn: fn(e))
        self.meta[eng].append(("i", inc, step if inc else 0, getattr(self, "ctx", "?")))
        self.n_instr += 1
        if eng in "VS" and (drain or SIM_FP64):
            self.cmds[eng].append(lambda e: e.drain())
            self.meta[eng].append(("i", None, 0, "drain"))
        return tok

    def dma(self, eng, fn, sem):
        return self.op(eng, fn, inc=sem, step=16)

    def tbar(self):
        self.wait("T", "cV", self.n["cV"])
        self.wait("T", "cS", self.n["cS"])

    def deadlock_check(self):
        state = {k: 0 for k in self.n}
        ptr = {e: 0 for e in self.cmds}
        progress = True
        while progress:
            progress = False
            for e in self.cmds:
                while ptr[e] < len(self.meta[e]):
                    m = self.meta[e][ptr[e]]
                    if m[0] == "w":
                        if state.get(m[1], 0) < m[2]:
                            break
                    elif m[1] is not None:
                        state[m[1]] = state.get(m[1], 0) + m[2]
                    ptr[e] += 1
                    progress = True
        stuck = {e: (ptr[e], len(self.meta[e])) for e in self.cmds if ptr[e] < len(self.meta[e])}
        detail = {e: self.meta[e][ptr[e]] for e in stuck}
        assert not stuck, f"DEADLOCK in schedule: {stuck}; waits={detail}; sems={state}"

    def build(self):
        nc = self.nc
        self.d_x = nc.dram_tensor("x", [BPC, S, D], F32, kind="ExternalInput")
        self.d_wcat = nc.dram_tensor("wcat", [D, H * 192], F32, kind="ExternalInput")
        self.d_wab = nc.dram_tensor("wab", [D, 48], F32, kind="ExternalInput")
        self.d_ab = nc.dram_tensor("ab", [48], F32, kind="ExternalInput")
        self.d_w0 = nc.dram_tensor("w0", [HD, HID], F32, kind="ExternalInput")
        self.d_w1 = nc.dram_tensor("w1", [HID, HD], F32, kind="ExternalInput")
        self.d_w1t = nc.dram_tensor("w1t", [HD, HID], F32, kind="ExternalInput")
        self.d_knw = nc.dram_tensor("knw", [HD], F32, kind="ExternalInput")
        self.d_qnw = nc.dram_tensor("qnw", [HD], F32, kind="ExternalInput")
        self.d_snw = nc.dram_tensor("snw", [D], F32, kind="ExternalInput")
        self.d_rnw = nc.dram_tensor("rnw", [D], F32, kind="ExternalInput")
        self.d_wqt = nc.dram_tensor("wqt8", [D, D], I8, kind="ExternalInput")
        self.d_wsc = nc.dram_tensor("wsc", [D], F32, kind="ExternalInput")
        self.d_fb = nc.dram_tensor("fb", [D], F32, kind="ExternalInput")
        self.d_ident = nc.dram_tensor("ident", [128, 128], BF16, kind="ExternalInput")
        self.d_out = nc.dram_tensor("out", [BPC, S, D], F32, kind="ExternalOutput")

        import contextlib
        with contextlib.ExitStack() as stk:
            self._alloc(stk)
            self._emit()
            self.deadlock_check()
            blk = stk.enter_context(nc.Block())

            @blk.sync
            def _(e):
                for fn in self.cmds["Q"]:
                    fn(e)

            @blk.gpsimd
            def _(e):
                for fn in self.cmds["P"]:
                    fn(e)

            @blk.tensor
            def _(e):
                for fn in self.cmds["T"]:
                    fn(e)

            @blk.vector
            def _(e):
                for fn in self.cmds["V"]:
                    fn(e)

            @blk.scalar
            def _(e):
                for fn in self.cmds["S"]:
                    fn(e)

        return nc

    def _alloc(self, stk):
        nc = self.nc
        sb = lambda n, s, d: stk.enter_context(nc.sbuf_tensor(n, s, d))
        ps = lambda n, s, d: stk.enter_context(nc.psum_tensor(n, s, d))
        for name in ["sK", "sW0", "sW1", "sX0", "sX1", "sAT", "sPW", "sA0", "sA1",
                     "sC0", "sC1", "sDB", "cT", "cV", "cS"]:
            self.sems[name] = stk.enter_context(nc.semaphore(name))
            self.n[name] = 0

        self.wcatb = sb("wcatb", [128, 8 * 3072], BF16)
        self.wqtb = sb("wqtb", [128, 8 * 1024], BF16)
        self.wabb = sb("wabb", [128, 8 * 48], BF16)
        self.w0b = sb("w0b", [64, 256], BF16)
        self.w1b = sb("w1b", [128, 128], BF16)
        self.w1tb = sb("w1tb", [64, 256], BF16)
        self.w0f = sb("w0f", [64, 256], F32)
        self.w1f = sb("w1f", [128, 128], F32)
        self.identb = sb("identb", [128, 128], BF16)
        self.knwf = sb("knwf", [128, 64], F32)
        self.qnwf = sb("qnwf", [128, 64], F32)
        self.abf = sb("abf", [128, 48], F32)
        self.wscf = sb("wscf", [128, 1024], F32)
        self.biasf = sb("biasf", [128, 1024], F32)
        self.snwc = sb("snwc", [128, 8], F32)
        self.rnwc = sb("rnwc", [128, 8], F32)
        self.onesf = sb("onesf", [128, 1], F32)
        self.ones1 = sb("ones1", [1, 128], F32)
        self.gflag = sb("gflag", [128, 1], F32)
        self.m52g = sb("m52g", [128, 1], F32)
        self.gm1f = sb("gm1f", [128, 1], F32)
        self.c127 = sb("c127", [128, 1], F32)
        self.ams = sb("ams", [1, 32], F32)
        self.w0pb = sb("w0pb", [64, 256], BF16)
        self.w1pb = sb("w1pb", [128, 128], BF16)

        self.xnT = sb("xnT", [128, 8 * 2048], BF16)
        self.th2 = sb("th2", [128, 256], F32)
        self.amh = sb("amh", [128, 32], F32)
        self.scols = sb("scols", [128, 16 * 10], F32)
        sc = lambda i: Buf(self.scols, 16 * 10, i * 16, 16)
        (self.negsum, self.ssqc, self.mcol, self.nmcol, self.rstd,
         self.rsc, self.rs2, self.varc, self.xscl, self.tmpP0) = [sc(i) for i in range(10)]

        HCW = 16 * 20
        self.hcols = sb("hcols", [128, HCW], F32)
        hc = lambda i: Buf(self.hcols, HCW, i * 16, 16)
        (self.kqsq_k, self.kqsq_q, self.rikq_k, self.rikq_q, self.ckc, self.cqc,
         self.p1sq, self.rinv, self.dyv, self.rinvp, self.htmpa, self.htmpb,
         self._h12, self._h13, self._h14, self._h15,
         self.p1psq, self.htmpc, self.htmpd, self._h19) = [hc(i) for i in range(20)]
        self.kqsq = Buf(self.hcols, HCW, 0, 32)
        self.rikq = Buf(self.hcols, HCW, 2 * 16, 32)
        self.htmp32a = Buf(self.hcols, HCW, 12 * 16, 32)
        self.htmp32b = Buf(self.hcols, HCW, 14 * 16, 32)
        self.tiny = sb("tiny", [128, 8], F32)
        tn = lambda i: Buf(self.tiny, 8, i, 1)
        (self.ssq11, self.coef11, self.coefb, self.negcoef, self.onea,
         self.t11a, self.t11b, self.t11c) = [tn(i) for i in range(8)]

        AFW = 13360 + 96
        self.arF_t = sb("arF", [128, AFW], F32)
        arF = Arena(self.arF_t, AFW)
        self.kvqf = arF.alloc(3072)
        self.keysCf = arF.alloc(1024)
        self.qCf = arF.alloc(1024)
        self.pre1f = arF.alloc(1024)
        self.hnf = arF.alloc(1024)
        self.ch1 = arF.alloc(1024)
        self.ch2 = arF.alloc(1024)
        self.xtile = arF.alloc(2048)
        self.outt = arF.alloc(2048)
        self.sgin = arF.alloc(48)
        self.sgout = arF.alloc(48)
        self.wstg = arF.overlay(self.kvqf, 2 * 3072)
        self.attnf = arF.overlay(self.xtile, 1024)
        self.qCf2 = arF.overlay(self.xtile, 1024, extra=1024)
        self.curb = arF.overlay(self.kvqf, 2048, extra=1024)
        self.pre1pf = arF.overlay(self.outt, 1024)
        self.wtmp = arF.overlay(self.outt, 256, extra=1024)

        ABW = 20992
        self.arB_t = sb("arB", [128, ABW], BF16)
        arB = Arena(self.arB_t, ABW)
        self.keysCb = arB.alloc(1024)
        self.valsCb = arB.alloc(1024)
        self.qCb = arB.alloc(1024)
        self.keysT = arB.alloc(2048)
        self.qT = arB.alloc(2048)
        self.act1b = arB.alloc(4096)
        self.act1Tb = arB.alloc(4096)
        self.ggb = arB.alloc(4096)
        self.dhb = arB.alloc(1024)
        self.dh0b = arB.alloc(512)
        self.dh0full = arB.overlay(self.act1Tb, 4096)
        self.dhT = arB.overlay(self.keysT, 2048)
        self.xnb = arB.overlay(self.dhb, 1024)
        self.xqbt = arB.overlay(self.dhb, 1024)
        self.xqTt = arB.overlay(self.qT, 2048)

        self.B0 = ps("B0", [128, 512], F32)
        self.B1 = ps("B1", [128, 512], F32)
        self.B2 = ps("B2", [128, 512], F32)
        self.B3 = ps("B3", [128, 512], F32)
        self.B4 = ps("B4", [128, 512], F32)
        self.B5 = ps("B5", [128, 512], F32)
        self.BT1 = ps("BT1", [128, 1024], BF16)
        self.BT0 = ps("BT0", [128, 1024], BF16)

    @staticmethod
    def _tiny_out(out):
        try:
            pairs = list(out.ap)
            n = 1
            for p in pairs[1:]:
                n *= int(p[1])
            return n < 256
        except Exception:
            return True

    def v_ts(self, out, in0, s1, s2=None, op0=ALU.mult, op1=ALU.bypass, inc=None, drain=None):
        drain = self._tiny_out(out) if drain is None else drain
        return self.op("V", lambda e: e.tensor_scalar(out, in0, s1, s2, op0, op1),
                       inc=inc, drain=drain)

    def v_tt(self, out, in0, in1, alu=ALU.mult, inc=None, drain=None):
        drain = self._tiny_out(out) if drain is None else drain
        return self.op("V", lambda e: e.tensor_tensor(out, in0, in1, alu), inc=inc, drain=drain)

    def v_stt(self, out, in0, s, in1, op0, op1, inc=None, drain=None):
        drain = self._tiny_out(out) if drain is None else drain
        return self.op("V", lambda e: e.scalar_tensor_tensor(out, in0, s, in1, op0, op1),
                       inc=inc, drain=drain)

    def v_copy(self, out, in_, inc=None, drain=None):
        drain = self._tiny_out(out) if drain is None else drain
        return self.op("V", lambda e: e.tensor_copy(out, in_), inc=inc, drain=drain)

    def s_copy(self, out, in_, inc="cS"):
        return self.op("S", lambda e: e.activation(out, in_, AF.Copy), inc=inc)

    def wtok(self, eng, tok):
        if tok:
            self.wait(eng, tok[0], tok[1])

    def newton_rsqrt(self, y, a, tmp, iters=2):
        for _ in range(iters):
            self.v_tt(tmp, y, y)
            self.v_tt(tmp, tmp, a)
            self.v_ts(tmp, tmp, -0.5, 1.5, ALU.mult, ALU.add)
            self.v_tt(y, y, tmp)

    def rsqrt_chain(self, dst, arg, tmp1, tmp2, arg_tok=None):
        self.wait("S", "cV", arg_tok if arg_tok else self.n["cV"])
        stok = self.op("S", lambda e: e.activation(tmp1, arg, AF.Sqrt), inc="cS")
        self.wait("V", "cS", stok)
        self.op("V", lambda e: e.reciprocal(dst, tmp1), drain=True)
        self.newton_rsqrt(dst, arg, tmp2)

    def _emit(self):
        self.prep()
        for s in range(BPC):
            self.p0(s)
            self.bank01_free = None
            self.bank456_free = None
            self.l_last = None
            self.g_last = 0
            self.p1A(s, 0)
            self.p1B(s, 0)
            self.p1C(s, 0)
            self.p1D(s, 0)
            self.p1EF(s, 0)
            for h in range(H):
                self.p1G(s, h)
                self.p1H(s, h)
                if h + 1 < H:
                    self.p1A(s, h + 1)
                self.p1I(s, h)
                self.p1J(s, h)
                self.p1K(s, h)
                if h >= 1:
                    self.p1M(s, h - 1)
                    self.p1N(s, h - 1)
                if h + 1 < H:
                    self.p1B(s, h + 1)
                    self.p1C(s, h + 1)
                self.p1L(s, h)
                if h + 1 < H:
                    self.p1D(s, h + 1)
                    self.p1EF(s, h + 1)
            self.p1M(s, H - 1)
            self.p1N(s, H - 1)
            self.p2(s)
        for sem in ["sPW", "sA0", "sA1", "sDB"]:
            self.wait("P", sem, self.n[sem])
        for sem in ["sK", "sW0", "sW1", "sX0", "sX1", "sAT", "sC0", "sC1"]:
            self.wait("Q", sem, self.n[sem])

    def prep(self):
        d = self
        self.ctx = 'prep' 
        snw_src = bass.AP(d.d_snw, 0, [[1, 128], [128, 8]])
        rnw_src = bass.AP(d.d_rnw, 0, [[1, 128], [128, 8]])
        knw_src = bass.AP(d.d_knw, 0, [[0, 128], [1, 64]])
        qnw_src = bass.AP(d.d_qnw, 0, [[0, 128], [1, 64]])
        ab_src = bass.AP(d.d_ab, 0, [[0, 128], [1, 48]])
        wsc_src = bass.AP(d.d_wsc, 0, [[0, 128], [1, 1024]])
        fb_src = bass.AP(d.d_fb, 0, [[0, 128], [1, 1024]])
        w0_src = bass.AP(d.d_w0, 0, [[256, 64], [1, 256]])
        w1_src = bass.AP(d.d_w1, 0, [[64, 128], [64 * 128, 2], [1, 64]])
        w1t_src = bass.AP(d.d_w1t, 0, [[256, 64], [1, 256]])
        id_src = bass.AP(d.d_ident, 0, [[128, 128], [1, 128]])

        for dst, src in [(d.snwc[:], snw_src), (d.rnwc[:], rnw_src), (d.knwf[:], knw_src),
                         (d.qnwf[:], qnw_src), (d.abf[:], ab_src), (d.wscf[:], wsc_src),
                         (d.biasf[:], fb_src), (d.w0f[:], w0_src),
                         (bass.AP(d.w1f, 0, [[128, 128], [64, 2], [1, 64]]), w1_src),
                         (d.identb[:], id_src)]:
            self.dma("Q", lambda e, dst=dst, src=src: e.dma_start(
                out=dst, in_=src, allow_slow_non_contiguous=True), "sK")
        q_small = self.n["sK"]
        for dst, src in [(d.w0b[:], w0_src),
                         (bass.AP(d.w1b, 0, [[128, 128], [64, 2], [1, 64]]), w1_src),
                         (d.w1tb[:], w1t_src)]:
            self.dma("P", lambda e, dst=dst, src=src: e.dma_start(out=dst, in_=src), "sPW")
        for i in range(8):
            src = bass.AP(d.d_wqt, i * 128 * 1024, [[1024, 128], [1, 1024]])
            self.dma("P", lambda e, src=src, i=i: e.dma_start(
                out=d.wqtb[:, i * 1024:(i + 1) * 1024], in_=src), "sPW")
        self.prep_done_p = self.n["sPW"]

        self.op("V", lambda e: e.memset(d.onesf[:], 1.0))
        self.op("V", lambda e: e.memset(d.ones1[:], 1.0))
        self.op("V", lambda e: e.memset(d.gflag[:], 1.0))
        self.op("V", lambda e: e.memset(d.c127[:], 127.0))
        self.v_ts(d.gflag[:], d.gflag[:], float(2.0 ** -40), -1.0, ALU.add, ALU.add)
        self.v_ts(d.gflag[:], d.gflag[:], float(2.0 ** 40), None, ALU.mult)
        self.v_ts(d.gm1f[:], d.gflag[:], 1.0, -1.0, ALU.subtract, ALU.mult)
        self.v_ts(d.m52g.ap(), d.gflag[:], -M52, None, ALU.mult)
        self.op("V", lambda e: e.reciprocal(d.c127[:], d.c127[:]), drain=True)
        self.v_ts(d.tiny.ap(0, 1) if False else d.t11a.ap(0, 1), d.c127[:], -127.0, 2.0,
                  ALU.mult, ALU.add)
        self.v_tt(d.c127[:], d.c127[:], d.t11a.ap(0, 1))

        self.wait("V", "sK", q_small)
        fold_tok = [0, 0]
        for i in range(8):
            half = i % 2
            src = bass.AP(d.d_wcat, i * 128 * 3072, [[3072, 128], [1, 3072]])
            if fold_tok[half]:
                self.wait("Q", "cV", fold_tok[half])
            tok = self.dma("Q", lambda e, src=src, half=half: e.dma_start(
                out=d.wstg.ap(half * 3072, 3072), in_=src), "sW%d" % half)
            self.wait("V", "sW%d" % half, tok)
            out_kv = bass.AP(d.wcatb, i * 3072, [[8 * 3072, 128], [192, 16], [1, 128]])
            in_kv = d.wstg.ap3(half * 3072, [(192, 16), (1, 128)])
            self.v_ts(out_kv, in_kv, d.snwc[:, i:i + 1], None, ALU.mult)
            out_q = bass.AP(d.wcatb, i * 3072 + 128, [[8 * 3072, 128], [192, 16], [1, 64]])
            in_q = d.wstg.ap3(half * 3072 + 128, [(192, 16), (1, 64)])
            fold_tok[half] = self.v_ts(out_q, in_q, d.rnwc[:, i:i + 1], None, ALU.mult, inc="cV")
        wab_src = bass.AP(d.d_wab, 0, [[48, 128], [128 * 48, 8], [1, 48]])
        self.wait("Q", "cV", fold_tok[0])
        self.wait("Q", "cV", fold_tok[1])
        tok = self.dma("Q", lambda e: e.dma_start(
            out=d.wstg.ap3(0, [(48, 8), (1, 48)]), in_=wab_src), "sW0")
        self.wait("V", "sW0", tok)
        snw_bc = bass.AP(d.snwc, 0, [[8, 128], [1, 8], [0, 48]])
        self.v_tt(bass.AP(d.wabb, 0, [[8 * 48, 128], [48, 8], [1, 48]]),
                  d.wstg.ap3(0, [(48, 8), (1, 48)]), snw_bc, ALU.mult, inc="cV")
        self.prep_done_q = q_small

    def p0(self, s):
        d = self
        self.ctx = 'p0s%d' % s
        xt = [d.xtile.ap(0, 1024), d.xtile.ap(1024, 1024)]
        self.tbar()
        self.wait("Q", "cV", self.n["cV"])
        self.wait("V", "sA0", self.n["sA0"])
        self.wait("V", "sA1", self.n["sA1"])
        if s == 0:
            self.wait("T", "sK", self.prep_done_q)
            self.wait("T", "sPW", self.prep_done_p)
        v_tok = [0] * T
        s_tok = [0] * T
        for t in range(T):
            if t >= 2:
                self.wait("Q", "cV", v_tok[t - 2])
                self.wait("Q", "cS", s_tok[t - 2])
            src = bass.AP(d.d_x, s * S * D + t * 128 * D, [[D, 128], [1, 1024]])
            sem = "sX%d" % (t % 2)
            ld = self.dma("Q", lambda e, src=src, t=t: e.dma_start(out=xt[t % 2], in_=src), sem)
            self.wait("V", sem, ld)
            v_tok[t] = self.op("V", lambda e, t=t: e.tensor_reduce(
                d.negsum.ap(t, 1), xt[t % 2], AX.X, ALU.add, negate=True), inc="cV",
                drain=(t == T - 1))
            self.wait("S", sem, ld)
            s_tok[t] = self.op("S", lambda e, t=t: e.activation(
                d.ch2.ap(), xt[t % 2], AF.Square, accum_out=d.ssqc.ap(t, 1)), inc="cS")
        self.wait("V", "cS", self.n["cS"])
        self.v_ts(d.mcol.ap(), d.negsum.ap(), -1.0 / 1024.0, None, ALU.mult)
        self.v_ts(d.nmcol.ap(), d.mcol.ap(), -1.0, None, ALU.mult)
        self.v_tt(d.tmpP0.ap(), d.mcol.ap(), d.mcol.ap())
        self.v_ts(d.varc.ap(), d.ssqc.ap(), 1.0 / 1024.0, None, ALU.mult)
        self.v_tt(d.varc.ap(), d.varc.ap(), d.tmpP0.ap(), ALU.subtract)
        tk = self.v_ts(d.tmpP0.ap(), d.varc.ap(), 1e-5, None, ALU.add, inc="cV")
        self.rsqrt_chain(d.rstd.ap(), d.tmpP0.ap(), d.htmpa.ap(), d.htmpb.ap(), arg_tok=tk)
        self.v_tt(d.htmpa.ap(), d.rstd.ap(), d.rstd.ap())
        self.v_tt(d.htmpa.ap(), d.htmpa.ap(), d.varc.ap())
        tk = self.v_ts(d.htmpa.ap(), d.htmpa.ap(), 1e-6, None, ALU.add, inc="cV")
        self.rsqrt_chain(d.rsc.ap(), d.htmpa.ap(), d.htmpb.ap(), d.tmpP0.ap(), arg_tok=tk)
        stats_tok = self.v_tt(d.rs2.ap(), d.rsc.ap(), d.rsc.ap(), inc="cV")
        if s == 0:
            self.dump("stats", d.scols[:], [128, 160])
        self.wait("Q", "cV", stats_tok)
        BAB = [d.B0, d.B1]
        xn_tok = [0] * T
        tr_tok = [0] * T
        ev_tok = [0] * T
        ab_tok = [0] * T
        sg_tok = [0] * T
        th_tok = [0] * T
        mm_tok = [0] * T
        for t in range(T):
            if t >= 2:
                self.wait("Q", "cV", xn_tok[t - 2])
            src = bass.AP(d.d_x, s * S * D + t * 128 * D, [[D, 128], [1, 1024]])
            sem = "sX%d" % (t % 2)
            ld = self.dma("Q", lambda e, src=src, t=t: e.dma_start(out=xt[t % 2], in_=src), sem)
            self.wait("V", sem, ld)
            xn_tok[t] = self.v_ts(d.xnb.ap(), xt[t % 2], d.mcol.ap(t, 1), d.rstd.ap(t, 1),
                                  ALU.subtract, ALU.mult, inc="cV")
            self.wait("T", "cV", xn_tok[t])
            halves = [([d.xnb.ap((4 * k + j) * 128, 128) for j in range(4)],
                       bass.AP(d.xnT, t * 128 + k * 4 * 2048,
                               [[8 * 2048, 128], [2048, 4], [1, 128]])) for k in range(2)]
            ev_tok[t] = self.transpose_halves(halves, pn=128)
            tr_tok[t] = ev_tok[t]
            self.wait("T", "cS", ev_tok[t])
            if t >= 2:
                self.wait("T", "cS", sg_tok[t - 2])
                self.wait("T", "cV", th_tok[t - 2])
            def abmm(e, t=t):
                ins = None
                for dd in range(8):
                    ins = e.matmul(BAB[t % 2][:, 0:48],
                                   bass.AP(d.xnT, dd * 2048 + t * 128, [[8 * 2048, 128], [1, 128]]),
                                   d.wabb[:, dd * 48:(dd + 1) * 48],
                                   start=(dd == 0), stop=(dd == 7))
                return ins
            ab_tok[t] = self.op("T", abmm, inc="cT")
            self.wait("V", "cT", ab_tok[t])
            if t >= 1:
                self.wait("V", "cS", sg_tok[t - 1])
            sg_in_tok = self.v_tt(d.sgin.ap(0, 48), BAB[t % 2][:, 0:48], d.abf[:],
                                  ALU.add, inc="cV", drain=False)
            self.wait("S", "cV", sg_in_tok)
            if t >= 1:
                self.wait("S", "cT", mm_tok[t - 1])
            sg_tok[t] = self.op("S", lambda e: e.activation(
                d.sgout.ap(0, 48), d.sgin.ap(0, 48), AF.Sigmoid), inc="cS")
            self.wait("V", "cS", sg_tok[t])
            th_tok[t] = self.v_ts(bass.AP(d.th2, t * 16, [[256, 128], [1, 16]]),
                                  d.sgout.ap(16, 16), 0.02, None, ALU.mult, inc="cV",
                                  drain=False)
            self.wait("T", "cS", sg_tok[t])
            mm_tok[t] = self.op("T", lambda e, t=t: e.matmul(
                d.B5[0:1, 0:32], d.onesf[:], d.sgout.ap3(0, [(32, 2), (1, 16)]),
                start=(t == 0), stop=(t == T - 1)), inc="cT")
        self.wait("V", "cT", mm_tok[T - 1])
        sc_tok = self.v_ts(bass.AP(d.ams, 0, [[32, 1], [1, 32]]), d.B5[0:1, 0:32],
                           1.0 / 2048.0, None, ALU.mult, inc="cV")
        self.wait("T", "cV", sc_tok)
        bc_tok = self.op("T", lambda e: e.matmul(
            d.B5[:, 32:64], d.ones1[:], bass.AP(d.ams, 0, [[32, 1], [1, 32]]),
            start=True, stop=True), inc="cT")
        self.wait("V", "cT", bc_tok)
        self.v_copy(d.amh[:], d.B5[:, 32:64], inc="cV")
        if s == 0:
            self.dump("xnT", d.xnT[:], [128, 16384])
            self.dump("th2", d.th2[:], [128, 256])
            self.dump("amh", d.amh[:], [128, 32])

    def p1A(self, s, h):
        self.p1A_start(s, h)
        self.p1A_tiles(s, h, 0, T)

    def p1A_start(self, s, h):
        if h == 0:
            self.tbar()
        self.a_ev = [0] * T
        if self.mat_tok:
            self.wait("S", "cV", self.mat_tok)

    def p1A_tiles(self, s, h, lo, hi):
        d = self
        self.ctx = 'p1Ah%d' % h
        BKV = [d.B0, d.B1]
        ev_tok = self.a_ev
        for t in range(lo, hi):
            if t == 0 and self.bank01_free:
                self.wtok("T", self.bank01_free)
            if t >= 2:
                self.wait("T", "cS", ev_tok[t - 2])
            def kvqmm(e, t=t):
                ins = None
                for dd in range(8):
                    ins = e.matmul(BKV[t % 2][:, 0:192],
                                   bass.AP(d.xnT, dd * 2048 + t * 128, [[8 * 2048, 128], [1, 128]]),
                                   d.wcatb[:, dd * 3072 + h * 192: dd * 3072 + (h + 1) * 192],
                                   start=(dd == 0), stop=(dd == 7))
                return ins
            mm = self.op("T", kvqmm, inc="cT")
            self.wait("S", "cT", mm)
            ev_tok[t] = self.s_copy(d.kvqf.ap(t * 192, 192), BKV[t % 2][:, 0:192])
        if hi == T:
            self.a_last = ("cS", ev_tok[T - 1])
        if s == 0 and h == 0 and hi == T:
            self.dump("kvqf", d.kvqf.ap(), [128, 3072])

    def p1B(self, s, h):
        d = self
        self.ctx = 'p1Bh%d' % h
        junk3 = d.ch2.ap3(0, [(64, 16), (1, 64)])
        hjunk3 = d.hnf.ap3(0, [(64, 16), (1, 64)])
        kAP0 = d.kvqf.ap3(0, [(192, 16), (1, 64)])
        qAP0 = d.kvqf.ap3(128, [(192, 16), (1, 64)])
        if self.dh_tok:
            self.wait("S", "cV", self.dh_tok)
        sk = self.op("S", lambda e: e.activation(junk3, kAP0, AF.Square), inc="cS")
        sq_ = self.op("S", lambda e: e.activation(hjunk3, qAP0, AF.Square), inc="cS")
        self.wait("V", "cS", sk)
        self.op("V", lambda e: e.tensor_reduce(d.kqsq_k.ap3(0, [(1, 16), (0, 1)]), junk3,
                                               AX.X, ALU.add), drain=False)
        self.wait("V", "cS", sq_)
        self.b_red_tok = self.op("V", lambda e: e.tensor_reduce(
            d.kqsq_q.ap3(0, [(1, 16), (0, 1)]), hjunk3, AX.X, ALU.add), inc="cV", drain=True)
        rs2bc = d.rs2.ap3(0, [(0, 2), (1, 16)])
        self.v_ts(d.kqsq.ap(), d.kqsq.ap(), 1.0 / 64.0, None, ALU.mult)
        self.v_tt(d.kqsq.ap(), d.kqsq.ap(), rs2bc)
        tk = self.v_ts(d.kqsq.ap(), d.kqsq.ap(), 1e-6, None, ALU.add, inc="cV")
        self.rsqrt_chain(d.rikq.ap(), d.kqsq.ap(), d.htmp32a.ap(), d.htmp32b.ap(), arg_tok=tk)
        self.v_tt(d.ckc.ap(), d.rikq_k.ap(), d.rsc.ap())
        self.v_tt(d.cqc.ap(), d.rikq_q.ap(), d.rsc.ap())

    def qCf_buf(self, h):
        return self.qCf if h % 2 == 0 else self.qCf2

    def p1C(self, s, h):
        d = self
        self.ctx = 'p1Ch%d' % h
        if self.g_last:
            self.wait("V", "cT", self.g_last)
        kAP = d.kvqf.ap3(0, [(192, 16), (1, 64)])
        vAP = d.kvqf.ap3(64, [(192, 16), (1, 64)])
        qAP = d.kvqf.ap3(128, [(192, 16), (1, 64)])
        qb = self.qCf_buf(h)
        kCf3 = d.keysCf.ap3(0, [(64, 16), (1, 64)])
        qCf3 = qb.ap3(0, [(64, 16), (1, 64)])
        self.v_tt(kCf3, kAP, d.ckc.ap3(0, [(1, 16), (0, 64)]))
        self.v_tt(d.keysCb.ap3(0, [(64, 16), (1, 64)]), kCf3,
                  bass.AP(d.knwf, 0, [[64, 128], [0, 16], [1, 64]]))
        self.v_tt(d.valsCb.ap3(0, [(64, 16), (1, 64)]), vAP, d.rsc.ap3(0, [(1, 16), (0, 64)]))
        self.v_tt(qCf3, qAP, d.cqc.ap3(0, [(1, 16), (0, 64)]))
        self.v_tt(qCf3, qCf3, bass.AP(d.qnwf, 0, [[64, 128], [0, 16], [1, 64]]))
        self.mat_tok = self.v_copy(d.qCb.ap(), qb.ap(), inc="cV")
        if s == 0 and h == 0:
            self.dump("keysCf", d.keysCf.ap(), [128, 1024])
            self.dump("qCf", qb.ap(), [128, 1024])
            self.dump("valsCb", d.valsCb.ap(), [128, 1024])

    def transpose_halves(self, halves, pn=64):
        d = self
        tok = None
        for (aps, dst_ap) in halves:
            hf = self.bt_half
            self.bt_half ^= 1
            if self.bt_free_h[hf]:
                self.wait("T", "cS", self.bt_free_h[hf])
            buf = d.BT0 if hf == 0 else d.BT1
            def trs(e, aps=aps, pn=pn, buf=buf):
                ins = None
                for i, in_ap in enumerate(aps):
                    ins = e.transpose(buf[0:pn, i * 128:(i + 1) * 128], in_ap, d.identb[:])
                return ins
            tk = self.op("T", trs, inc="cT")
            self.wait("S", "cT", tk)
            tok = self.s_copy(dst_ap, buf[0:pn, 0:512])
            self.bt_free_h[hf] = tok
        self.bt_free = tok
        return tok

    def p1D(self, s, h):
        d = self
        self.ctx = 'p1Dh%d' % h
        self.wait("T", "cV", self.mat_tok)
        for (srcb, dstb) in ((d.keysCb, d.keysT), (d.qCb, d.qT)):
            for g in range(2):
                halves = [([srcb.ap((g * 8 + 4 * k + j) * 64, 64) for j in range(4)],
                           dstb.ap(g * 1024 + k * 512, 512, pn=64)) for k in range(2)]
                self.transpose_halves(halves, pn=64)
            if dstb is d.keysT:
                self.keysT_ready = self.bt_free
        if s == 0 and h == 0:
            self.dump("keysT", d.keysT.ap(), [128, 2048])

    def p1EF(self, s, h):
        d = self
        self.ctx = 'p1EFh%d' % h
        B01 = [d.B0, d.B1]
        NP_ = T // 2
        f_toks = [0] * 8
        self.wait("T", "cS", self.keysT_ready)
        self.wtok("T", self.a_last)
        for u in range(8):
            bank = [d.B0, d.B1][u % 2]
            if u >= 2:
                self.wait("T", "cS", f_toks[u - 2])
            def p0T(e, u=u, bank=bank):
                return e.matmul(bank[:, 0:512],
                                d.w0b[0:64, (u % 2) * 128:(u % 2) * 128 + 128],
                                d.keysT.ap((u // 2) * 512, 512, pn=64), start=True, stop=True)
            mm = self.op("T", p0T, inc="cT")
            self.wait("S", "cT", mm)
            f_toks[u] = self.op("S", lambda e, u=u, bank=bank: e.activation(
                d.act1Tb.ap((u % 2) * 2048 + (u // 2) * 512, 512), bank[:, 0:512], AF.Gelu),
                inc="cS")
        self.act1T_ready = f_toks[7]
        self.f_toks = f_toks
        if s == 0 and h == 0:
            self.dump("act1Tb", d.act1Tb.ap(), [128, 4096])
        act_tok = [None] * NP_
        self.wait("T", "cS", f_toks[7])
        for p in range(NP_):
            if p >= 2:
                self.wait("T", act_tok[p - 2][0], act_tok[p - 2][1])
            def e_mm(e, p=p):
                e.matmul(B01[p % 2][:, 0:256], d.keysT.ap(2 * p * 128, 128, pn=64), d.w0b[:],
                         start=True, stop=True)
                return e.matmul(B01[p % 2][:, 256:512], d.keysT.ap((2 * p + 1) * 128, 128, pn=64),
                                d.w0b[:], start=True, stop=True)
            mm = self.op("T", e_mm, inc="cT")
            self.wait("S", "cT", mm)
            self.op("S", lambda e, p=p: e.activation(d.act1b.ap(p * 512, 512),
                                                     B01[p % 2][:, 0:512], AF.Gelu))
            act_tok[p] = self.emit_gg(B01[p % 2][:, 0:512], d.ggb.ap(p * 512, 512), mm)
        self.bank01_free = act_tok[NP_ - 1]
        if s == 0 and h == 0:
            self.dump("act1b", d.act1b.ap(), [128, 4096])
            self.dump("ggb", d.ggb.ap(), [128, 4096])

    def p1G(self, s, h):
        d = self
        self.ctx = 'p1Gh%d' % h
        B23 = [d.B2, d.B3]
        NG = T // 4
        p1_tok = [0] * NG
        if self.dh_tok:
            self.wait("S", "cV", self.dh_tok)
        if self.l_last:
            self.wtok("T", self.l_last)
        for g in range(NG):
            self.wait("T", "cS", self.f_toks[2 * g + 1])
            if g >= 2:
                self.wait("T", "cS", p1_tok[g - 2])
            def p1mm(e, g=g):
                ins = None
                for j in range(4):
                    t = 4 * g + j
                    e.matmul(B23[g % 2][:, j * 64:(j + 1) * 64], d.act1Tb.ap(t * 128, 128),
                             d.w1b[:, 0:64], start=True, stop=False)
                    ins = e.matmul(B23[g % 2][:, j * 64:(j + 1) * 64],
                                   d.act1Tb.ap(2048 + t * 128, 128),
                                   d.w1b[:, 64:128], start=False, stop=True)
                return ins
            mm = self.op("T", p1mm, inc="cT")
            self.wait("S", "cT", mm)
            p1_tok[g] = self.s_copy(d.pre1f.ap(g * 256, 256), B23[g % 2][:, 0:256])
        self.g_toks = ("cS", p1_tok[NG - 1])

    def p1H(self, s, h):
        d = self
        self.ctx = 'p1Hh%d' % h
        junk3 = d.ch2.ap3(0, [(64, 16), (1, 64)])
        p13 = d.pre1f.ap3(0, [(64, 16), (1, 64)])
        self.wtok("V", self.g_toks)
        self.v_tt(junk3, p13, p13)
        self.h_red_tok = self.op("V", lambda e: e.tensor_reduce(
            d.p1sq.ap3(0, [(1, 16), (0, 1)]), junk3, AX.X, ALU.add), inc="cV", drain=True)
        tk = self.v_ts(d.htmpa.ap(), d.p1sq.ap(), 1.0 / 64.0, 1e-8, ALU.mult, ALU.add, inc="cV")
        self.rsqrt_chain(d.rinv.ap(), d.htmpa.ap(), d.htmpb.ap(), d.p1sq.ap(), arg_tok=tk)
        if s == 0 and h == 0:
            self.dump("pre1f", d.pre1f.ap(), [128, 1024])
            self.dump("hcols", d.hcols[:], [128, 256])

    def p1I(self, s, h):
        d = self
        self.ctx = 'p1Ih%d' % h
        rinv_bc = d.rinv.ap3(0, [(1, 16), (0, 64)])
        pre1_3 = d.pre1f.ap3(0, [(64, 16), (1, 64)])
        hn3 = d.hnf.ap3(0, [(64, 16), (1, 64)])
        ch1_3 = d.ch1.ap3(0, [(64, 16), (1, 64)])
        ch2_3 = d.ch2.ap3(0, [(64, 16), (1, 64)])
        th2_2d = bass.AP(d.th2, h, [[256, 128], [16, 16]])
        self.v_tt(hn3, pre1_3, rinv_bc)
        self.v_tt(d.ch1.ap(), d.hnf.ap(), d.keysCb.ap(), ALU.add)
        self.v_tt(d.ch1.ap(), d.ch1.ap(), d.valsCb.ap(), ALU.subtract)
        self.v_tt(d.ch2.ap(), d.ch1.ap(), d.hnf.ap())
        self.op("V", lambda e: e.tensor_reduce(d.dyv.ap3(0, [(1, 16), (0, 1)]),
                                               ch2_3, AX.X, ALU.add), drain=True)
        self.v_tt(d.htmpa.ap(), th2_2d, d.rinv.ap())
        self.v_ts(d.dyv.ap(), d.dyv.ap(), 1.0 / 64.0, None, ALU.mult)
        self.v_tt(d.htmpb.ap(), d.dyv.ap(), d.htmpa.ap())
        self.v_tt(ch2_3, hn3, d.htmpb.ap3(0, [(1, 16), (0, 64)]))
        self.v_tt(ch1_3, ch1_3, d.htmpa.ap3(0, [(1, 16), (0, 64)]))
        self.dh_tok = self.v_tt(d.dhb.ap(), d.ch1.ap(), d.ch2.ap(), ALU.subtract, inc="cV")
        if s == 0 and h == 0:
            self.dump("dhb", d.dhb.ap(), [128, 1024])

    def p1J(self, s, h, a_cb=None):
        d = self
        self.ctx = 'p1Jh%d' % h
        B23 = [d.B2, d.B3]
        self.wait("T", "cV", self.dh_tok)
        for g in range(2):
            halves = [([d.dhb.ap((g * 8 + 4 * k + j) * 64, 64) for j in range(4)],
                       d.dhT.ap(g * 1024 + k * 512, 512, pn=64)) for k in range(2)]
            self.transpose_halves(halves, pn=64)
        self.wait("T", "cS", self.bt_free)
        if self.bank456_free:
            self.wait("T", "cV", self.bank456_free[0])
            self.wait("T", "cS", self.bank456_free[1])
        NP_ = T // 2
        mult_tok = [0] * NP_
        for p in range(NP_):
            if p >= 2:
                self.wait("T", "cV", mult_tok[p - 2])
            def dh0mm(e, p=p):
                e.matmul(B23[p % 2][:, 0:256], d.dhT.ap(2 * p * 128, 128, pn=64), d.w1tb[:],
                         start=True, stop=True)
                return e.matmul(B23[p % 2][:, 256:512], d.dhT.ap((2 * p + 1) * 128, 128, pn=64),
                                d.w1tb[:], start=True, stop=True)
            dh0 = self.op("T", dh0mm, inc="cT")
            def g1mm(e, p=p):
                ins = None
                for j in range(2):
                    t = 2 * p + j
                    e.matmul(d.B4[:, 0:64], d.act1b.ap(t * 256, 128), d.dhb.ap(t * 64, 64),
                             start=(t == 0), stop=(t == T - 1))
                    ins = e.matmul(d.B4[:, 64:128], d.act1b.ap(t * 256 + 128, 128),
                                   d.dhb.ap(t * 64, 64), start=False, stop=(t == T - 1))
                return ins
            self.op("T", g1mm)
            self.wait("V", "cT", dh0)
            mult_tok[p] = self.v_tt(d.dh0full.ap(p * 512, 512), B23[p % 2][:, 0:512],
                                    d.ggb.ap(p * 512, 512), inc="cV")
            if a_cb is not None:
                a_cb(p)
                self.ctx = 'p1Jh%d' % h
        self.wait("T", "cV", mult_tok[NP_ - 1])
        def g0mm(e):
            ins = None
            for t in range(T):
                ins = e.matmul(d.B5[0:64, 0:256], d.keysCb.ap(t * 64, 64),
                               d.dh0full.ap((t // 2) * 512 + (t % 2) * 256, 256),
                               start=(t == 0), stop=(t == T - 1))
            return ins
        g_last = self.op("T", g0mm, inc="cT")
        self.j_mult_last = (mult_tok[NP_ - 1], mult_tok[NP_ - 2])
        self.g_last = g_last

    def p1K(self, s, h):
        d = self
        self.ctx = 'p1Kh%d' % h
        self.wait("S", "cT", self.g_last)
        if self.b_red_tok:
            self.wait("S", "cV", self.b_red_tok)
        self.op("S", lambda e: e.activation(d.ch2.ap(0, 256, pn=64), d.B5[0:64, 0:256], AF.Square,
                                            accum_out=d.t11a.ap(0, 1, pn=64)))
        self.op("S", lambda e: e.activation(d.ch2.ap(0, 64), d.B4[:, 0:64], AF.Square,
                                            accum_out=d.t11b.ap(0, 1)))
        ssq_tok = self.op("S", lambda e: e.activation(d.ch2.ap(0, 64), d.B4[:, 64:128], AF.Square,
                                                      accum_out=d.t11c.ap(0, 1)), inc="cS")
        self.wait("T", "cS", ssq_tok)
        def ssqmm(e):
            e.matmul(d.B5[0:1, 480:481], d.onesf[0:64, 0:1], d.t11a.ap(0, 1, pn=64),
                     start=True, stop=False)
            e.matmul(d.B5[0:1, 480:481], d.onesf[:], d.t11b.ap(0, 1), start=False, stop=False)
            return e.matmul(d.B5[0:1, 480:481], d.onesf[:], d.t11c.ap(0, 1),
                            start=False, stop=True)
        mm = self.op("T", ssqmm, inc="cT")
        self.wait("S", "cT", mm)
        sq = self.op("S", lambda e: e.activation(d.t11a.ap(0, 1, pn=1), d.B5[0:1, 480:481],
                                                 AF.Sqrt), inc="cS")
        self.wait("V", "cS", sq)
        self.v_ts(d.t11a.ap(0, 1, pn=1), d.t11a.ap(0, 1, pn=1), 1e-6, None, ALU.add)
        self.op("V", lambda e: e.reciprocal(d.t11b.ap(0, 1, pn=1), d.t11a.ap(0, 1, pn=1)),
                drain=True)
        self.v_stt(d.t11c.ap(0, 1, pn=1), d.t11a.ap(0, 1, pn=1), -1.0, d.t11b.ap(0, 1, pn=1),
                   ALU.mult, ALU.mult)
        self.v_ts(d.t11c.ap(0, 1, pn=1), d.t11c.ap(0, 1, pn=1), 2.0, None, ALU.add)
        self.v_tt(d.t11b.ap(0, 1, pn=1), d.t11b.ap(0, 1, pn=1), d.t11c.ap(0, 1, pn=1))
        cf = self.v_ts(d.coef11.ap(0, 1, pn=1), d.t11b.ap(0, 1, pn=1), 10.0, 1.0,
                       ALU.mult, ALU.min, inc="cV")
        self.wait("T", "cV", cf)
        bc = self.op("T", lambda e: e.matmul(d.B5[:, 481:482], d.ones1[:],
                                             d.coef11.ap(0, 1, pn=1), start=True, stop=True),
                     inc="cT")
        self.wait("V", "cT", bc)
        self.v_copy(d.coefb.ap(0, 1), d.B5[:, 481:482])
        self.v_ts(d.negcoef.ap(0, 1), d.coefb.ap(0, 1), -1.0, None, ALU.mult)
        self.v_ts(d.onea.ap(0, 1), d.amh[:, h:h + 1], 1.0, -1.0, ALU.subtract, ALU.mult)
        self.v_ts(d.wtmp.ap(0, 256, pn=64), d.w0f[:], d.onea.ap(0, 1, pn=64), None, ALU.mult,
                  drain=False)
        self.v_stt(d.w0pb[:], d.B5[0:64, 0:256], d.negcoef.ap(0, 1, pn=64),
                   d.wtmp.ap(0, 256, pn=64), ALU.mult, ALU.add)
        self.v_ts(d.wtmp.ap(0, 128), d.w1f[:], d.onea.ap(0, 1), None, ALU.mult, drain=False)
        self.v_stt(d.w1pb[:, 0:64], d.B4[:, 0:64], d.negcoef.ap(0, 1),
                   d.wtmp.ap(0, 64), ALU.mult, ALU.add)
        wp_tok = self.v_stt(d.w1pb[:, 64:128], d.B4[:, 64:128], d.negcoef.ap(0, 1),
                            d.wtmp.ap(64, 64), ALU.mult, ALU.add, inc="cV")
        self.wp_tok = wp_tok
        self.ssq_tok = ssq_tok
        self.bank456_free = (wp_tok, ssq_tok)
        if s == 0 and h == 0:
            self.dump("w0pb", d.w0pb[:], [64, 256])
            self.dump("w1pb", d.w1pb[:], [128, 128])
            self.dump("coefb", d.coefb.ap(0, 1), [128, 1])

    def p1L(self, s, h):
        d = self
        self.ctx = 'p1Lh%d' % h
        B23 = [d.B2, d.B3]
        self.wait("T", "cV", self.wp_tok)
        self.wait("T", "cS", self.ssq_tok)
        f_toks = [0] * 8
        for u in range(8):
            bank = [d.B4, d.B5][u % 2]
            if u >= 2:
                self.wait("T", "cS", f_toks[u - 2])
            def p0pT(e, u=u, bank=bank):
                return e.matmul(bank[:, 0:512],
                                d.w0pb[0:64, (u % 2) * 128:(u % 2) * 128 + 128],
                                d.qT.ap((u // 2) * 512, 512, pn=64), start=True, stop=True)
            mm = self.op("T", p0pT, inc="cT")
            self.wait("S", "cT", mm)
            f_toks[u] = self.op("S", lambda e, u=u, bank=bank: e.activation(
                d.act1Tb.ap((u % 2) * 2048 + (u // 2) * 512, 512), bank[:, 0:512], AF.Gelu),
                inc="cS")
        NG = T // 4
        p1_tok = [0] * NG
        if self.n_at:
            self.wait("S", "cV", self.n_at)
        self.wait("T", "cS", f_toks[7])
        self.wait("T", "cV", self.j_mult_last[0])
        self.wait("T", "cV", self.j_mult_last[1])
        for g in range(NG):
            if g >= 2:
                self.wait("T", "cS", p1_tok[g - 2])
            def p1pmm(e, g=g):
                ins = None
                for j in range(4):
                    t = 4 * g + j
                    e.matmul(B23[g % 2][:, j * 64:(j + 1) * 64], d.act1Tb.ap(t * 128, 128),
                             d.w1pb[:, 0:64], start=True, stop=False)
                    ins = e.matmul(B23[g % 2][:, j * 64:(j + 1) * 64],
                                   d.act1Tb.ap(2048 + t * 128, 128),
                                   d.w1pb[:, 64:128], start=False, stop=True)
                return ins
            mm = self.op("T", p1pmm, inc="cT")
            self.wait("S", "cT", mm)
            p1_tok[g] = self.s_copy(d.pre1pf.ap(g * 256, 256), B23[g % 2][:, 0:256])
        self.l_last = ("cS", p1_tok[NG - 1])

    def p1M(self, s, h):
        d = self
        self.ctx = 'p1Mh%d' % h
        junk3 = d.ch2.ap3(0, [(64, 16), (1, 64)])
        pp3 = d.pre1pf.ap3(0, [(64, 16), (1, 64)])
        if self.dh_tok:
            self.wait("S", "cV", self.dh_tok)
        self.wtok("S", self.l_last)
        msq = self.op("S", lambda e: e.activation(junk3, pp3, AF.Square), inc="cS")
        self.wait("V", "cS", msq)
        self.op("V", lambda e: e.tensor_reduce(d.p1psq.ap3(0, [(1, 16), (0, 1)]), junk3,
                                               AX.X, ALU.add), drain=True)
        tk = self.v_ts(d.htmpc.ap(), d.p1psq.ap(), 1.0 / 64.0, 1e-8, ALU.mult, ALU.add, inc="cV")
        self.rsqrt_chain(d.rinvp.ap(), d.htmpc.ap(), d.htmpd.ap(), d.p1psq.ap(), arg_tok=tk)

    def p1N(self, s, h):
        d = self
        self.ctx = 'p1Nh%d' % h
        if self.attn_tok:
            self.wait("V", "sAT", self.attn_tok)
        qb = self.qCf_buf(h)
        self.v_tt(d.attnf.ap3(0, [(64, 16), (1, 64)]), d.pre1pf.ap3(0, [(64, 16), (1, 64)]),
                  d.rinvp.ap3(0, [(1, 16), (0, 64)]))
        at = self.n_at = self.v_tt(d.attnf.ap(), d.attnf.ap(), qb.ap(), ALU.add, inc="cV")
        if s == 0 and h == 0:
            self.dump("attnf", d.attnf.ap(), [128, 1024])
        self.wait("Q", "cV", at)
        dst = bass.AP(self.d_out, s * S * D + h * 64, [[D, 128], [128 * D, 16], [1, 64]])
        self.attn_tok = self.dma("Q", lambda e, dst=dst: e.dma_start(
            out=dst, in_=d.attnf.ap3(0, [(64, 16), (1, 64)])), "sAT")

    def emit_gg(self, psum_ap, out_ap, mm_tok):
        d = self
        if not SIM_FP64:
            tok = self.op("S", lambda e: e.activation(out_ap, psum_ap, AF.Derivative_Gelu),
                          inc="cS")
            return ("cS", tok)
        W = 256
        gel = d.ch1.ap(0, W)
        e1 = d.hnf.ap(0, W)
        e2 = d.pre1f.ap(0, W)
        m = d.outt.ap(0, W)
        tok = self.op("S", lambda e: e.activation(gel, psum_ap, AF.Gelu), inc="cS")
        self.wait("V", "cS", tok)
        self.v_ts(m, psum_ap, 0.0, None, ALU.abs_max)
        self.v_ts(m, m, 1e-30, None, ALU.is_ge)
        self.v_ts(e1, m, 1.0, None, ALU.subtract)
        self.v_tt(e2, psum_ap, e1, ALU.subtract)
        self.v_tt(gel, gel, e2, ALU.divide)
        self.v_tt(gel, gel, m)
        self.v_ts(e1, e1, -0.5, None, ALU.mult)
        self.v_tt(gel, gel, e1, ALU.add)
        self.v_tt(e1, psum_ap, psum_ap)
        tok = self.v_copy(e1, e1, inc="cV")
        self.wait("S", "cV", tok)
        tok = self.op("S", lambda e: e.activation(e2, e1, AF.Exp, scale=-0.5), inc="cS")
        self.wait("V", "cS", tok)
        self.v_tt(e2, e2, psum_ap)
        chi = float(np.float32(0.3989422804014327))
        clo = float(np.float64(0.3989422804014327) - np.float64(np.float32(0.3989422804014327)))
        self.v_ts(e1, e2, chi, None, ALU.mult)
        self.v_tt(gel, gel, e1, ALU.add)
        self.v_ts(e1, e2, clo, None, ALU.mult)
        tok = self.v_tt(out_ap, gel, e1, ALU.add, inc="cV")
        return ("cV", tok)

    def p2(self, s):
        d = self
        self.ctx = 'p2s%d' % s
        xt = [d.xtile.ap(0, 1024), d.xtile.ap(1024, 1024)]
        self.tbar()
        self.wait("Q", "cV", self.n["cV"])
        attn_done = self.n["sAT"]
        xc_tok = [0] * T
        q_tok_l = [0] * T
        hnf_free = [0] * T
        xb_tok = [0] * T
        ev_tok = [0] * T
        ffn_tok = [0] * T
        acc_tok = [0] * T
        cur_tok = [0] * T
        cur_used = [0] * T
        for t in range(T):
            if t >= 2:
                self.wait("Q", "cV", xc_tok[t - 2])
                self.wait("Q", "cV", xb_tok[t - 2])
            src = bass.AP(d.d_x, s * S * D + t * 128 * D, [[D, 128], [1, 1024]])
            sem = "sX%d" % (t % 2)
            ld = self.dma("Q", lambda e, src=src, t=t: e.dma_start(out=xt[t % 2], in_=src), sem)
            self.wait("V", sem, ld)
            xc_tok[t] = self.v_ts(d.ch1.ap(), xt[t % 2], d.mcol.ap(t, 1), None,
                                  ALU.subtract, inc="cV")
            a = d.htmpa.ap(0, 1)
            b = d.htmpb.ap(0, 1)
            r = d.htmpa.ap(1, 1)
            cmb = d.htmpb.ap(1, 1)
            self.op("V", lambda e: e.reduce_max(a, d.ch1.ap(), AX.X,
                                                apply_absolute_value=True), drain=True)
            self.v_tt(b, a, d.rstd.ap(t, 1))
            self.v_ts(b, b, 1e-6, None, ALU.max)
            self.op("V", lambda e: e.reciprocal(r, b), drain=True)
            self.v_stt(cmb, b, -1.0, r, ALU.mult, ALU.mult)
            self.v_ts(cmb, cmb, 2.0, None, ALU.add)
            self.v_tt(r, r, cmb)
            self.v_tt(d.xscl.ap(t, 1), b, d.c127[:])
            self.v_ts(r, r, 127.0, None, ALU.mult)
            self.v_tt(cmb, r, d.rstd.ap(t, 1))
            sc_tok = self.v_ts(d.ch2.ap(), d.ch1.ap(), cmb, None, ALU.mult, inc="cV")
            self.wait("S", "cV", sc_tok)
            if t >= 1:
                self.wait("S", "cV", hnf_free[t - 1])
            self.op("S", lambda e: e.activation(d.keysCf.ap(), d.ch2.ap(), AF.Copy,
                                                bias=float(M32)))
            self.op("S", lambda e: e.activation(d.hnf.ap(), d.keysCf.ap(), AF.Copy,
                                                bias=float(-M32)))
            u64cp = self.op("S", lambda e: e.activation(d.keysCf.ap(), d.ch2.ap(), AF.Copy,
                                                        bias=float(M52)), inc="cS")
            self.wait("V", "cS", u64cp)
            self.v_ts(d.pre1f.ap(), d.keysCf.ap(), d.gflag[:], d.m52g.ap(),
                      ALU.mult, ALU.add)
            q_tok = q_tok_l[t] = self.v_stt(d.xqbt.ap(), d.hnf.ap(), d.gm1f[:], d.pre1f.ap(),
                                            ALU.mult, ALU.add, inc="cV")
            self.wait("T", "cV", q_tok)
            halves = [([d.xqbt.ap((4 * k + j) * 128, 128) for j in range(4)],
                       d.xqTt.ap((t % 2) * 1024 + k * 512, 512)) for k in range(2)]
            ev_tok[t] = self.transpose_halves(halves, pn=128)
            self.wait("T", "cS", ev_tok[t])
            if t >= 1:
                self.wait("T", "cV", ffn_tok[t - 1])
            def ffnmm(e, t=t):
                ins = None
                for j in range(8):
                    lhsT = d.xqTt.ap((t % 2) * 1024 + j * 128, 128)
                    e.matmul(d.B0[:, 0:512], lhsT, d.wqtb[:, j * 1024: j * 1024 + 512],
                             start=(j == 0), stop=(j == 7))
                    ins = e.matmul(d.B1[:, 0:512], lhsT,
                                   d.wqtb[:, j * 1024 + 512: (j + 1) * 1024],
                                   start=(j == 0), stop=(j == 7))
                return ins
            mm = self.op("T", ffnmm, inc="cT")
            if t == 0:
                self.wait("Q", "sAT", attn_done)
            if t >= 2:
                self.wait("Q", "cV", cur_used[t - 2])
            srcc = bass.AP(self.d_out, s * S * D + t * 128 * D, [[D, 128], [1, 1024]])
            cur_tok[t] = self.dma("Q", lambda e, srcc=srcc, t=t: e.dma_start(
                out=d.curb.ap((t % 2) * 1024, 1024), in_=srcc), "sC%d" % (t % 2))
            self.wait("V", "cT", mm)
            if t >= 2:
                self.wait("V", "sA%d" % (t % 2), acc_tok[t - 2])
            xb_tok[t] = self.v_tt(d.ch2.ap(), xt[t % 2], d.biasf[:], ALU.add, inc="cV")
            xscol = d.xscl.ap(t, 1)
            self.v_tt(d.hnf.ap(0, 512), d.B0[:, 0:512], d.wscf[:, 0:512])
            self.v_stt(d.outt.ap((t % 2) * 1024, 512), d.hnf.ap(0, 512), xscol,
                       d.ch2.ap(0, 512), ALU.mult, ALU.add)
            self.v_tt(d.hnf.ap(512, 512), d.B1[:, 0:512], d.wscf[:, 512:1024])
            hnf_free[t] = self.v_stt(d.outt.ap((t % 2) * 1024 + 512, 512), d.hnf.ap(512, 512),
                                     xscol, d.ch2.ap(512, 512), ALU.mult, ALU.add, inc="cV")
            self.wait("V", "sC%d" % (t % 2), cur_tok[t])
            ffn_tok[t] = cur_used[t] = self.v_tt(
                d.outt.ap((t % 2) * 1024, 1024), d.outt.ap((t % 2) * 1024, 1024),
                d.curb.ap((t % 2) * 1024, 1024), ALU.add, inc="cV")
            self.wait("P", "cV", ffn_tok[t])
            dstp = bass.AP(self.d_out, s * S * D + t * 128 * D, [[D, 128], [1, 1024]])
            acc_tok[t] = self.dma("P", lambda e, dstp=dstp, t=t: e.dma_start(
                out=dstp, in_=d.outt.ap((t % 2) * 1024, 1024)), "sA%d" % (t % 2))
        if s == 0:
            self.dump("xqbt15", d.xqbt.ap(), [128, 1024])
            self.dump("outt", d.outt.ap(), [128, 2048])


def _build_and_run(shards, host):
    mk = MK()
    nc = mk.build()
    in_maps = []
    for xc in shards:
        m = dict(host)
        m["x"] = xc
        in_maps.append(m)
    res = run_bass_kernel_spmd(nc, in_maps, core_ids=list(range(len(shards))))
    return np.concatenate([r["out"] for r in res.results], axis=0)


def kernel(L0_attn_x, L0_attn_W_K, L0_attn_W_V, L0_attn_W_Q, L0_attn_mem_W0, L0_attn_mem_W1,
           L0_attn_k_norm_w, L0_attn_q_norm_w, L0_attn_store_norm_w, L0_attn_retrieve_norm_w,
           L0_attn_alpha_w, L0_attn_alpha_b, L0_attn_theta_w, L0_attn_theta_b,
           L0_attn_eta_w, L0_attn_eta_b, L0_ffn_weight_q, L0_ffn_weight_scale, L0_ffn_bias):
    f32 = lambda a: np.ascontiguousarray(np.asarray(a), dtype=np.float32)
    x = f32(L0_attn_x)
    wk, wv, wq = f32(L0_attn_W_K), f32(L0_attn_W_V), f32(L0_attn_W_Q)
    wcat = np.empty((D, H, 192), np.float32)
    wcat[:, :, 0:64] = wk.reshape(D, H, 64)
    wcat[:, :, 64:128] = wv.reshape(D, H, 64)
    wcat[:, :, 128:192] = wq.reshape(D, H, 64)
    wcat = np.ascontiguousarray(wcat.reshape(D, H * 192))
    wab = np.ascontiguousarray(np.concatenate(
        [f32(L0_attn_alpha_w), f32(L0_attn_theta_w), f32(L0_attn_eta_w)], axis=1))
    ab = np.ascontiguousarray(np.concatenate(
        [f32(L0_attn_alpha_b), f32(L0_attn_theta_b), f32(L0_attn_eta_b)]))
    w1 = f32(L0_attn_mem_W1)
    host = {
        "wcat": wcat, "wab": wab, "ab": ab,
        "w0": f32(L0_attn_mem_W0), "w1": w1,
        "w1t": np.ascontiguousarray(w1.T),
        "knw": f32(L0_attn_k_norm_w), "qnw": f32(L0_attn_q_norm_w),
        "snw": f32(L0_attn_store_norm_w), "rnw": f32(L0_attn_retrieve_norm_w),
        "wqt8": np.ascontiguousarray(np.asarray(L0_ffn_weight_q).T),
        "wsc": f32(L0_ffn_weight_scale), "fb": f32(L0_ffn_bias),
        "ident": np.eye(128, dtype=np.float32).astype(ml_dtypes.bfloat16),
    }
    shards = [np.ascontiguousarray(x[c * BPC:(c + 1) * BPC]) for c in range(NCORE)]
    out = _build_and_run(shards, host)
    return out


# version 38
# speedup vs baseline: 1.0421x; 1.0421x over previous
import os
import numpy as np
import ml_dtypes
import concourse.bass as bass
import concourse.mybir as mybir
from concourse.bass_utils import run_bass_kernel_spmd

F32 = mybir.dt.float32
BF16 = mybir.dt.bfloat16
I8 = mybir.dt.int8
AF = mybir.ActivationFunctionType
ALU = mybir.AluOpType
AX = mybir.AxisListType

M32 = 12582912.0
M52 = 6755399441055744.0

NCORE = 8
BPC = 2
S, D, H, HD, HID = 2048, 1024, 16, 64, 256
T = S // 128

SIM_FP64 = os.environ.get("BASS_SIM_FP64") == "1"


class Buf:

    def __init__(self, t, W, off, w):
        self.t, self.W, self.off, self.w = t, W, off, w

    def ap(self, c0=0, cw=None, p0=0, pn=128):
        cw = self.w - c0 if cw is None else cw
        return bass.AP(self.t, p0 * self.W + self.off + c0, [[self.W, pn], [1, cw]])

    def ap3(self, c0, pairs, p0=0, pn=128):
        return bass.AP(self.t, p0 * self.W + self.off + c0,
                       [[self.W, pn]] + [[st, c] for (st, c) in pairs])


class Arena:
    def __init__(self, t, W):
        self.t, self.W, self.off = t, W, 0

    def alloc(self, w):
        assert self.off + w <= self.W, f"arena overflow: {self.off}+{w} > {self.W}"
        b = Buf(self.t, self.W, self.off, w)
        self.off += w
        return b

    def overlay(self, at, w, extra=0):
        assert at.off + extra + w <= self.W
        return Buf(self.t, self.W, at.off + extra, w)


DEBUG = False


class MK:
    def __init__(self, debug=False):
        self.nc = bass.Bass("TRN2", target_bir_lowering=False)
        self.cmds = {k: [] for k in "QPTVS"}
        self.meta = {k: [] for k in "QPTVS"}
        self.n = {}
        self.sems = {}
        self.n_instr = 0
        self.attn_tok = 0
        self.bt_free = 0
        self.bt_free_h = [0, 0]
        self.bt_half = 0
        self.l_gelu_tok = 0
        self.n_at = 0
        self.mat_tok = 0
        self.dh_tok = 0
        self.b_red_tok = 0
        self.h_red_tok = 0
        self.keysT_ready = 0
        self.debug = debug or DEBUG
        self.dump_names = []

    def dump(self, name, src_ap, shape):
        if not self.debug:
            return
        dt = nc_dt = F32
        dout = self.nc.dram_tensor("dbg_" + name, list(shape), nc_dt, kind="ExternalOutput")
        self.dump_names.append("dbg_" + name)
        self.wait("P", "cV", self.n["cV"])
        self.wait("P", "cS", self.n["cS"])
        self.wait("P", "cT", self.n["cT"])
        tok = self.dma("P", lambda e: e.dma_start(
            out=bass.AP(dout, 0, [[shape[1], shape[0]], [1, shape[1]]]), in_=src_ap), "sDB")
        for eng in "TVSQ":
            self.wait(eng, "sDB", tok)

    def wait(self, eng, sem, val):
        if val <= 0:
            return
        h = self.sems[sem]
        self.cmds[eng].append(lambda e, h=h, v=val: e.wait_ge(h, v))
        self.meta[eng].append(("w", sem, val, getattr(self, "ctx", "?")))

    def op(self, eng, fn, inc=None, step=1, drain=False):
        tok = None
        if inc is not None:
            self.n[inc] += step
            tok = self.n[inc]
            h = self.sems[inc]
            self.cmds[eng].append(lambda e, fn=fn, h=h, st=step: fn(e).then_inc(h, st))
        else:
            self.cmds[eng].append(lambda e, fn=fn: fn(e))
        self.meta[eng].append(("i", inc, step if inc else 0, getattr(self, "ctx", "?")))
        self.n_instr += 1
        if eng in "VS" and (drain or SIM_FP64):
            self.cmds[eng].append(lambda e: e.drain())
            self.meta[eng].append(("i", None, 0, "drain"))
        return tok

    def dma(self, eng, fn, sem):
        return self.op(eng, fn, inc=sem, step=16)

    def tbar(self):
        self.wait("T", "cV", self.n["cV"])
        self.wait("T", "cS", self.n["cS"])

    def deadlock_check(self):
        state = {k: 0 for k in self.n}
        ptr = {e: 0 for e in self.cmds}
        progress = True
        while progress:
            progress = False
            for e in self.cmds:
                while ptr[e] < len(self.meta[e]):
                    m = self.meta[e][ptr[e]]
                    if m[0] == "w":
                        if state.get(m[1], 0) < m[2]:
                            break
                    elif m[1] is not None:
                        state[m[1]] = state.get(m[1], 0) + m[2]
                    ptr[e] += 1
                    progress = True
        stuck = {e: (ptr[e], len(self.meta[e])) for e in self.cmds if ptr[e] < len(self.meta[e])}
        detail = {e: self.meta[e][ptr[e]] for e in stuck}
        assert not stuck, f"DEADLOCK in schedule: {stuck}; waits={detail}; sems={state}"

    def build(self):
        nc = self.nc
        self.d_x = nc.dram_tensor("x", [BPC, S, D], F32, kind="ExternalInput")
        self.d_wcat = nc.dram_tensor("wcat", [D, H * 192], F32, kind="ExternalInput")
        self.d_wab = nc.dram_tensor("wab", [D, 48], F32, kind="ExternalInput")
        self.d_ab = nc.dram_tensor("ab", [48], F32, kind="ExternalInput")
        self.d_w0 = nc.dram_tensor("w0", [HD, HID], F32, kind="ExternalInput")
        self.d_w1 = nc.dram_tensor("w1", [HID, HD], F32, kind="ExternalInput")
        self.d_w1t = nc.dram_tensor("w1t", [HD, HID], F32, kind="ExternalInput")
        self.d_knw = nc.dram_tensor("knw", [HD], F32, kind="ExternalInput")
        self.d_qnw = nc.dram_tensor("qnw", [HD], F32, kind="ExternalInput")
        self.d_snw = nc.dram_tensor("snw", [D], F32, kind="ExternalInput")
        self.d_rnw = nc.dram_tensor("rnw", [D], F32, kind="ExternalInput")
        self.d_wqt = nc.dram_tensor("wqt8", [D, D], I8, kind="ExternalInput")
        self.d_wsc = nc.dram_tensor("wsc", [D], F32, kind="ExternalInput")
        self.d_fb = nc.dram_tensor("fb", [D], F32, kind="ExternalInput")
        self.d_ident = nc.dram_tensor("ident", [128, 128], BF16, kind="ExternalInput")
        self.d_out = nc.dram_tensor("out", [BPC, S, D], F32, kind="ExternalOutput")

        import contextlib
        with contextlib.ExitStack() as stk:
            self._alloc(stk)
            self._emit()
            self.deadlock_check()
            blk = stk.enter_context(nc.Block())

            @blk.sync
            def _(e):
                for fn in self.cmds["Q"]:
                    fn(e)

            @blk.gpsimd
            def _(e):
                for fn in self.cmds["P"]:
                    fn(e)

            @blk.tensor
            def _(e):
                for fn in self.cmds["T"]:
                    fn(e)

            @blk.vector
            def _(e):
                for fn in self.cmds["V"]:
                    fn(e)

            @blk.scalar
            def _(e):
                for fn in self.cmds["S"]:
                    fn(e)

        return nc

    def _alloc(self, stk):
        nc = self.nc
        sb = lambda n, s, d: stk.enter_context(nc.sbuf_tensor(n, s, d))
        ps = lambda n, s, d: stk.enter_context(nc.psum_tensor(n, s, d))
        for name in ["sK", "sW0", "sW1", "sX0", "sX1", "sAT", "sPW", "sA0", "sA1",
                     "sC0", "sC1", "sDB", "cT", "cV", "cS"]:
            self.sems[name] = stk.enter_context(nc.semaphore(name))
            self.n[name] = 0

        self.wcatb = sb("wcatb", [128, 8 * 3072], BF16)
        self.wqtb = sb("wqtb", [128, 8 * 1024], BF16)
        self.wabb = sb("wabb", [128, 8 * 48], BF16)
        self.w0b = sb("w0b", [64, 256], BF16)
        self.w1b = sb("w1b", [128, 128], BF16)
        self.w1tb = sb("w1tb", [64, 256], BF16)
        self.w0f = sb("w0f", [64, 256], F32)
        self.w1f = sb("w1f", [128, 128], F32)
        self.identb = sb("identb", [128, 128], BF16)
        self.knwf = sb("knwf", [128, 64], F32)
        self.qnwf = sb("qnwf", [128, 64], F32)
        self.abf = sb("abf", [128, 48], F32)
        self.wscf = sb("wscf", [128, 1024], F32)
        self.biasf = sb("biasf", [128, 1024], F32)
        self.snwc = sb("snwc", [128, 8], F32)
        self.rnwc = sb("rnwc", [128, 8], F32)
        self.onesf = sb("onesf", [128, 1], F32)
        self.ones1 = sb("ones1", [1, 128], F32)
        self.gflag = sb("gflag", [128, 1], F32)
        self.gm1f = sb("gm1f", [128, 1], F32)
        self.c127 = sb("c127", [128, 1], F32)
        self.ams = sb("ams", [1, 32], F32)
        self.w0pb = sb("w0pb", [64, 256], BF16)
        self.w1pb = sb("w1pb", [128, 128], BF16)

        self.xnT = sb("xnT", [128, 8 * 2048], BF16)
        self.th2 = sb("th2", [128, 256], F32)
        self.amh = sb("amh", [128, 32], F32)
        self.scols = sb("scols", [128, 16 * 10], F32)
        sc = lambda i: Buf(self.scols, 16 * 10, i * 16, 16)
        (self.negsum, self.ssqc, self.mcol, self.nmcol, self.rstd,
         self.rsc, self.rs2, self.varc, self.xscl, self.tmpP0) = [sc(i) for i in range(10)]

        HCW = 16 * 20
        self.hcols = sb("hcols", [128, HCW], F32)
        hc = lambda i: Buf(self.hcols, HCW, i * 16, 16)
        (self.kqsq_k, self.kqsq_q, self.rikq_k, self.rikq_q, self.ckc, self.cqc,
         self.p1sq, self.rinv, self.dyv, self.rinvp, self.htmpa, self.htmpb,
         self._h12, self._h13, self._h14, self._h15,
         self.p1psq, self.htmpc, self.htmpd, self._h19) = [hc(i) for i in range(20)]
        self.kqsq = Buf(self.hcols, HCW, 0, 32)
        self.rikq = Buf(self.hcols, HCW, 2 * 16, 32)
        self.htmp32a = Buf(self.hcols, HCW, 12 * 16, 32)
        self.htmp32b = Buf(self.hcols, HCW, 14 * 16, 32)
        self.tiny = sb("tiny", [128, 8], F32)
        tn = lambda i: Buf(self.tiny, 8, i, 1)
        (self.ssq11, self.coef11, self.coefb, self.negcoef, self.onea,
         self.t11a, self.t11b, self.t11c) = [tn(i) for i in range(8)]

        AFW = 13360 + 96
        self.arF_t = sb("arF", [128, AFW], F32)
        arF = Arena(self.arF_t, AFW)
        self.kvqf = arF.alloc(3072)
        self.keysCf = arF.alloc(1024)
        self.qCf = arF.alloc(1024)
        self.pre1f = arF.alloc(1024)
        self.hnf = arF.alloc(1024)
        self.ch1 = arF.alloc(1024)
        self.ch2 = arF.alloc(1024)
        self.xtile = arF.alloc(2048)
        self.outt = arF.alloc(2048)
        self.sgin = arF.alloc(48)
        self.sgout = arF.alloc(48)
        self.wstg = arF.overlay(self.kvqf, 2 * 3072)
        self.attnf = arF.overlay(self.xtile, 1024)
        self.qCf2 = arF.overlay(self.xtile, 1024, extra=1024)
        self.curb = arF.overlay(self.kvqf, 2048, extra=1024)
        self.pre1pf = arF.overlay(self.outt, 1024)
        self.wtmp = arF.overlay(self.outt, 256, extra=1024)

        ABW = 20992
        self.arB_t = sb("arB", [128, ABW], BF16)
        arB = Arena(self.arB_t, ABW)
        self.keysCb = arB.alloc(1024)
        self.valsCb = arB.alloc(1024)
        self.qCb = arB.alloc(1024)
        self.keysT = arB.alloc(2048)
        self.qT = arB.alloc(2048)
        self.act1b = arB.alloc(4096)
        self.act1Tb = arB.alloc(4096)
        self.ggb = arB.alloc(4096)
        self.dhb = arB.alloc(1024)
        self.dh0b = arB.alloc(512)
        self.dh0full = arB.overlay(self.act1Tb, 4096)
        self.dhT = arB.overlay(self.keysT, 2048)
        self.xnb = arB.overlay(self.dhb, 1024)
        self.xqbt = arB.overlay(self.dhb, 1024)
        self.xqTt = arB.overlay(self.qT, 2048)

        self.B0 = ps("B0", [128, 512], F32)
        self.B1 = ps("B1", [128, 512], F32)
        self.B2 = ps("B2", [128, 512], F32)
        self.B3 = ps("B3", [128, 512], F32)
        self.B4 = ps("B4", [128, 512], F32)
        self.B5 = ps("B5", [128, 512], F32)
        self.BT1 = ps("BT1", [128, 1024], BF16)
        self.BT0 = ps("BT0", [128, 1024], BF16)

    @staticmethod
    def _tiny_out(out):
        try:
            pairs = list(out.ap)
            n = 1
            for p in pairs[1:]:
                n *= int(p[1])
            return n < 256
        except Exception:
            return True

    def v_ts(self, out, in0, s1, s2=None, op0=ALU.mult, op1=ALU.bypass, inc=None, drain=None):
        drain = self._tiny_out(out) if drain is None else drain
        return self.op("V", lambda e: e.tensor_scalar(out, in0, s1, s2, op0, op1),
                       inc=inc, drain=drain)

    def v_tt(self, out, in0, in1, alu=ALU.mult, inc=None, drain=None):
        drain = self._tiny_out(out) if drain is None else drain
        return self.op("V", lambda e: e.tensor_tensor(out, in0, in1, alu), inc=inc, drain=drain)

    def v_stt(self, out, in0, s, in1, op0, op1, inc=None, drain=None):
        drain = self._tiny_out(out) if drain is None else drain
        return self.op("V", lambda e: e.scalar_tensor_tensor(out, in0, s, in1, op0, op1),
                       inc=inc, drain=drain)

    def v_copy(self, out, in_, inc=None, drain=None):
        drain = self._tiny_out(out) if drain is None else drain
        return self.op("V", lambda e: e.tensor_copy(out, in_), inc=inc, drain=drain)

    def s_copy(self, out, in_, inc="cS"):
        return self.op("S", lambda e: e.activation(out, in_, AF.Copy), inc=inc)

    def wtok(self, eng, tok):
        if tok:
            self.wait(eng, tok[0], tok[1])

    def newton_rsqrt(self, y, a, tmp, iters=1):
        for _ in range(iters):
            self.v_tt(tmp, y, y)
            self.v_tt(tmp, tmp, a)
            self.v_ts(tmp, tmp, -0.5, 1.5, ALU.mult, ALU.add)
            self.v_tt(y, y, tmp)

    def rsqrt_chain(self, dst, arg, tmp1, tmp2, arg_tok=None):
        self.wait("S", "cV", arg_tok if arg_tok else self.n["cV"])
        stok = self.op("S", lambda e: e.activation(tmp1, arg, AF.Sqrt), inc="cS")
        self.wait("V", "cS", stok)
        self.op("V", lambda e: e.reciprocal(dst, tmp1), drain=True)
        self.newton_rsqrt(dst, arg, tmp2)

    def _emit(self):
        self.prep()
        for s in range(BPC):
            self.p0(s)
            self.bank01_free = None
            self.bank456_free = None
            self.l_last = None
            self.g_last = 0
            self.p1A(s, 0)
            self.p1B(s, 0)
            self.p1C(s, 0)
            self.p1D(s, 0)
            self.p1EF(s, 0)
            for h in range(H):
                self.p1G(s, h)
                self.p1H(s, h)
                if h + 1 < H:
                    self.p1A(s, h + 1)
                self.p1I(s, h)
                self.p1J(s, h)
                self.p1K(s, h)
                if h >= 1:
                    self.p1M(s, h - 1)
                    self.p1N(s, h - 1)
                if h + 1 < H:
                    self.p1B(s, h + 1)
                    self.p1C(s, h + 1)
                self.p1L(s, h)
                if h + 1 < H:
                    self.p1D(s, h + 1)
                    self.p1EF(s, h + 1)
            self.p1M(s, H - 1)
            self.p1N(s, H - 1)
            self.p2(s)
        for sem in ["sPW", "sA0", "sA1", "sDB"]:
            self.wait("P", sem, self.n[sem])
        for sem in ["sK", "sW0", "sW1", "sX0", "sX1", "sAT", "sC0", "sC1"]:
            self.wait("Q", sem, self.n[sem])

    def prep(self):
        d = self
        self.ctx = 'prep' 
        snw_src = bass.AP(d.d_snw, 0, [[1, 128], [128, 8]])
        rnw_src = bass.AP(d.d_rnw, 0, [[1, 128], [128, 8]])
        knw_src = bass.AP(d.d_knw, 0, [[0, 128], [1, 64]])
        qnw_src = bass.AP(d.d_qnw, 0, [[0, 128], [1, 64]])
        ab_src = bass.AP(d.d_ab, 0, [[0, 128], [1, 48]])
        wsc_src = bass.AP(d.d_wsc, 0, [[0, 128], [1, 1024]])
        fb_src = bass.AP(d.d_fb, 0, [[0, 128], [1, 1024]])
        w0_src = bass.AP(d.d_w0, 0, [[256, 64], [1, 256]])
        w1_src = bass.AP(d.d_w1, 0, [[64, 128], [64 * 128, 2], [1, 64]])
        w1t_src = bass.AP(d.d_w1t, 0, [[256, 64], [1, 256]])
        id_src = bass.AP(d.d_ident, 0, [[128, 128], [1, 128]])

        for dst, src in [(d.snwc[:], snw_src), (d.rnwc[:], rnw_src), (d.knwf[:], knw_src),
                         (d.qnwf[:], qnw_src), (d.abf[:], ab_src), (d.wscf[:], wsc_src),
                         (d.biasf[:], fb_src), (d.w0f[:], w0_src),
                         (bass.AP(d.w1f, 0, [[128, 128], [64, 2], [1, 64]]), w1_src),
                         (d.identb[:], id_src)]:
            self.dma("Q", lambda e, dst=dst, src=src: e.dma_start(
                out=dst, in_=src, allow_slow_non_contiguous=True), "sK")
        q_small = self.n["sK"]
        for dst, src in [(d.w0b[:], w0_src),
                         (bass.AP(d.w1b, 0, [[128, 128], [64, 2], [1, 64]]), w1_src),
                         (d.w1tb[:], w1t_src)]:
            self.dma("P", lambda e, dst=dst, src=src: e.dma_start(out=dst, in_=src), "sPW")
        for i in range(8):
            src = bass.AP(d.d_wqt, i * 128 * 1024, [[1024, 128], [1, 1024]])
            self.dma("P", lambda e, src=src, i=i: e.dma_start(
                out=d.wqtb[:, i * 1024:(i + 1) * 1024], in_=src), "sPW")
        self.prep_done_p = self.n["sPW"]

        self.op("V", lambda e: e.memset(d.onesf[:], 1.0))
        self.op("V", lambda e: e.memset(d.ones1[:], 1.0))
        self.op("V", lambda e: e.memset(d.gflag[:], 1.0))
        self.op("V", lambda e: e.memset(d.c127[:], 127.0))
        self.v_ts(d.gflag[:], d.gflag[:], float(2.0 ** -40), -1.0, ALU.add, ALU.add)
        self.v_ts(d.gflag[:], d.gflag[:], float(2.0 ** 40), None, ALU.mult)
        self.v_ts(d.gm1f[:], d.gflag[:], 1.0, -1.0, ALU.subtract, ALU.mult)
        self.op("V", lambda e: e.reciprocal(d.c127[:], d.c127[:]), drain=True)
        self.v_ts(d.tiny.ap(0, 1) if False else d.t11a.ap(0, 1), d.c127[:], -127.0, 2.0,
                  ALU.mult, ALU.add)
        self.v_tt(d.c127[:], d.c127[:], d.t11a.ap(0, 1))

        self.wait("V", "sK", q_small)
        fold_tok = [0, 0]
        for i in range(8):
            half = i % 2
            src = bass.AP(d.d_wcat, i * 128 * 3072, [[3072, 128], [1, 3072]])
            if fold_tok[half]:
                self.wait("Q", "cV", fold_tok[half])
            tok = self.dma("Q", lambda e, src=src, half=half: e.dma_start(
                out=d.wstg.ap(half * 3072, 3072), in_=src), "sW%d" % half)
            self.wait("V", "sW%d" % half, tok)
            out_kv = bass.AP(d.wcatb, i * 3072, [[8 * 3072, 128], [192, 16], [1, 128]])
            in_kv = d.wstg.ap3(half * 3072, [(192, 16), (1, 128)])
            self.v_ts(out_kv, in_kv, d.snwc[:, i:i + 1], None, ALU.mult)
            out_q = bass.AP(d.wcatb, i * 3072 + 128, [[8 * 3072, 128], [192, 16], [1, 64]])
            in_q = d.wstg.ap3(half * 3072 + 128, [(192, 16), (1, 64)])
            fold_tok[half] = self.v_ts(out_q, in_q, d.rnwc[:, i:i + 1], None, ALU.mult, inc="cV")
        wab_src = bass.AP(d.d_wab, 0, [[48, 128], [128 * 48, 8], [1, 48]])
        self.wait("Q", "cV", fold_tok[0])
        self.wait("Q", "cV", fold_tok[1])
        tok = self.dma("Q", lambda e: e.dma_start(
            out=d.wstg.ap3(0, [(48, 8), (1, 48)]), in_=wab_src), "sW0")
        self.wait("V", "sW0", tok)
        snw_bc = bass.AP(d.snwc, 0, [[8, 128], [1, 8], [0, 48]])
        self.v_tt(bass.AP(d.wabb, 0, [[8 * 48, 128], [48, 8], [1, 48]]),
                  d.wstg.ap3(0, [(48, 8), (1, 48)]), snw_bc, ALU.mult, inc="cV")
        self.prep_done_q = q_small

    def p0(self, s):
        d = self
        self.ctx = 'p0s%d' % s
        xt = [d.xtile.ap(0, 1024), d.xtile.ap(1024, 1024)]
        self.tbar()
        self.wait("Q", "cV", self.n["cV"])
        self.wait("V", "sA0", self.n["sA0"])
        self.wait("V", "sA1", self.n["sA1"])
        if s == 0:
            self.wait("T", "sK", self.prep_done_q)
            self.wait("T", "sPW", self.prep_done_p)
        v_tok = [0] * T
        s_tok = [0] * T
        for t in range(T):
            if t >= 2:
                self.wait("Q", "cV", v_tok[t - 2])
                self.wait("Q", "cS", s_tok[t - 2])
            src = bass.AP(d.d_x, s * S * D + t * 128 * D, [[D, 128], [1, 1024]])
            sem = "sX%d" % (t % 2)
            ld = self.dma("Q", lambda e, src=src, t=t: e.dma_start(out=xt[t % 2], in_=src), sem)
            self.wait("V", sem, ld)
            v_tok[t] = self.op("V", lambda e, t=t: e.tensor_reduce(
                d.negsum.ap(t, 1), xt[t % 2], AX.X, ALU.add, negate=True), inc="cV",
                drain=(t == T - 1))
            self.wait("S", sem, ld)
            s_tok[t] = self.op("S", lambda e, t=t: e.activation(
                d.ch2.ap(), xt[t % 2], AF.Square, accum_out=d.ssqc.ap(t, 1)), inc="cS")
        self.wait("V", "cS", self.n["cS"])
        self.v_ts(d.mcol.ap(), d.negsum.ap(), -1.0 / 1024.0, None, ALU.mult)
        self.v_ts(d.nmcol.ap(), d.mcol.ap(), -1.0, None, ALU.mult)
        self.v_tt(d.tmpP0.ap(), d.mcol.ap(), d.mcol.ap())
        self.v_ts(d.varc.ap(), d.ssqc.ap(), 1.0 / 1024.0, None, ALU.mult)
        self.v_tt(d.varc.ap(), d.varc.ap(), d.tmpP0.ap(), ALU.subtract)
        tk = self.v_ts(d.tmpP0.ap(), d.varc.ap(), 1e-5, None, ALU.add, inc="cV")
        self.rsqrt_chain(d.rstd.ap(), d.tmpP0.ap(), d.htmpa.ap(), d.htmpb.ap(), arg_tok=tk)
        self.v_tt(d.htmpa.ap(), d.rstd.ap(), d.rstd.ap())
        self.v_tt(d.htmpa.ap(), d.htmpa.ap(), d.varc.ap())
        tk = self.v_ts(d.htmpa.ap(), d.htmpa.ap(), 1e-6, None, ALU.add, inc="cV")
        self.rsqrt_chain(d.rsc.ap(), d.htmpa.ap(), d.htmpb.ap(), d.tmpP0.ap(), arg_tok=tk)
        stats_tok = self.v_tt(d.rs2.ap(), d.rsc.ap(), d.rsc.ap(), inc="cV")
        if s == 0:
            self.dump("stats", d.scols[:], [128, 160])
        self.wait("Q", "cV", stats_tok)
        BAB = [d.B0, d.B1]
        xn_tok = [0] * T
        tr_tok = [0] * T
        ev_tok = [0] * T
        ab_tok = [0] * T
        sg_tok = [0] * T
        th_tok = [0] * T
        mm_tok = [0] * T
        for t in range(T):
            if t >= 2:
                self.wait("Q", "cV", xn_tok[t - 2])
            src = bass.AP(d.d_x, s * S * D + t * 128 * D, [[D, 128], [1, 1024]])
            sem = "sX%d" % (t % 2)
            ld = self.dma("Q", lambda e, src=src, t=t: e.dma_start(out=xt[t % 2], in_=src), sem)
            self.wait("V", sem, ld)
            xn_tok[t] = self.v_ts(d.xnb.ap(), xt[t % 2], d.mcol.ap(t, 1), d.rstd.ap(t, 1),
                                  ALU.subtract, ALU.mult, inc="cV")
            self.wait("T", "cV", xn_tok[t])
            halves = [([d.xnb.ap((4 * k + j) * 128, 128) for j in range(4)],
                       bass.AP(d.xnT, t * 128 + k * 4 * 2048,
                               [[8 * 2048, 128], [2048, 4], [1, 128]])) for k in range(2)]
            ev_tok[t] = self.transpose_halves(halves, pn=128)
            tr_tok[t] = ev_tok[t]
            self.wait("T", "cS", ev_tok[t])
            if t >= 2:
                self.wait("T", "cS", sg_tok[t - 2])
                self.wait("T", "cV", th_tok[t - 2])
            def abmm(e, t=t):
                ins = None
                for dd in range(8):
                    ins = e.matmul(BAB[t % 2][:, 0:48],
                                   bass.AP(d.xnT, dd * 2048 + t * 128, [[8 * 2048, 128], [1, 128]]),
                                   d.wabb[:, dd * 48:(dd + 1) * 48],
                                   start=(dd == 0), stop=(dd == 7))
                return ins
            ab_tok[t] = self.op("T", abmm, inc="cT")
            self.wait("V", "cT", ab_tok[t])
            if t >= 1:
                self.wait("V", "cS", sg_tok[t - 1])
            sg_in_tok = self.v_tt(d.sgin.ap(0, 48), BAB[t % 2][:, 0:48], d.abf[:],
                                  ALU.add, inc="cV", drain=False)
            self.wait("S", "cV", sg_in_tok)
            if t >= 1:
                self.wait("S", "cT", mm_tok[t - 1])
            sg_tok[t] = self.op("S", lambda e: e.activation(
                d.sgout.ap(0, 48), d.sgin.ap(0, 48), AF.Sigmoid), inc="cS")
            self.wait("V", "cS", sg_tok[t])
            th_tok[t] = self.v_ts(bass.AP(d.th2, t * 16, [[256, 128], [1, 16]]),
                                  d.sgout.ap(16, 16), 0.02, None, ALU.mult, inc="cV",
                                  drain=False)
            self.wait("T", "cS", sg_tok[t])
            mm_tok[t] = self.op("T", lambda e, t=t: e.matmul(
                d.B5[0:1, 0:32], d.onesf[:], d.sgout.ap3(0, [(32, 2), (1, 16)]),
                start=(t == 0), stop=(t == T - 1)), inc="cT")
        self.wait("V", "cT", mm_tok[T - 1])
        sc_tok = self.v_ts(bass.AP(d.ams, 0, [[32, 1], [1, 32]]), d.B5[0:1, 0:32],
                           1.0 / 2048.0, None, ALU.mult, inc="cV")
        self.wait("T", "cV", sc_tok)
        bc_tok = self.op("T", lambda e: e.matmul(
            d.B5[:, 32:64], d.ones1[:], bass.AP(d.ams, 0, [[32, 1], [1, 32]]),
            start=True, stop=True), inc="cT")
        self.wait("V", "cT", bc_tok)
        self.v_copy(d.amh[:], d.B5[:, 32:64], inc="cV")
        if s == 0:
            self.dump("xnT", d.xnT[:], [128, 16384])
            self.dump("th2", d.th2[:], [128, 256])
            self.dump("amh", d.amh[:], [128, 32])

    def p1A(self, s, h):
        self.p1A_start(s, h)
        self.p1A_tiles(s, h, 0, T)

    def p1A_start(self, s, h):
        if h == 0:
            self.tbar()
        self.a_ev = [0] * T
        if self.mat_tok:
            self.wait("S", "cV", self.mat_tok)

    def p1A_tiles(self, s, h, lo, hi):
        d = self
        self.ctx = 'p1Ah%d' % h
        BKV = [d.B0, d.B1]
        ev_tok = self.a_ev
        for t in range(lo, hi):
            if t == 0 and self.bank01_free:
                self.wtok("T", self.bank01_free)
            if t >= 2:
                self.wait("T", "cS", ev_tok[t - 2])
            def kvqmm(e, t=t):
                ins = None
                for dd in range(8):
                    ins = e.matmul(BKV[t % 2][:, 0:192],
                                   bass.AP(d.xnT, dd * 2048 + t * 128, [[8 * 2048, 128], [1, 128]]),
                                   d.wcatb[:, dd * 3072 + h * 192: dd * 3072 + (h + 1) * 192],
                                   start=(dd == 0), stop=(dd == 7))
                return ins
            mm = self.op("T", kvqmm, inc="cT")
            self.wait("S", "cT", mm)
            ev_tok[t] = self.s_copy(d.kvqf.ap(t * 192, 192), BKV[t % 2][:, 0:192])
        if hi == T:
            self.a_last = ("cS", ev_tok[T - 1])
        if s == 0 and h == 0 and hi == T:
            self.dump("kvqf", d.kvqf.ap(), [128, 3072])

    def p1B(self, s, h):
        d = self
        self.ctx = 'p1Bh%d' % h
        junk3 = d.ch2.ap3(0, [(64, 16), (1, 64)])
        hjunk3 = d.hnf.ap3(0, [(64, 16), (1, 64)])
        kAP0 = d.kvqf.ap3(0, [(192, 16), (1, 64)])
        qAP0 = d.kvqf.ap3(128, [(192, 16), (1, 64)])
        if self.dh_tok:
            self.wait("S", "cV", self.dh_tok)
        sk = self.op("S", lambda e: e.activation(junk3, kAP0, AF.Square), inc="cS")
        sq_ = self.op("S", lambda e: e.activation(hjunk3, qAP0, AF.Square), inc="cS")
        self.wait("V", "cS", sk)
        self.op("V", lambda e: e.tensor_reduce(d.kqsq_k.ap3(0, [(1, 16), (0, 1)]), junk3,
                                               AX.X, ALU.add), drain=False)
        self.wait("V", "cS", sq_)
        self.b_red_tok = self.op("V", lambda e: e.tensor_reduce(
            d.kqsq_q.ap3(0, [(1, 16), (0, 1)]), hjunk3, AX.X, ALU.add), inc="cV", drain=True)
        rs2bc = d.rs2.ap3(0, [(0, 2), (1, 16)])
        self.v_ts(d.kqsq.ap(), d.kqsq.ap(), 1.0 / 64.0, None, ALU.mult)
        self.v_tt(d.kqsq.ap(), d.kqsq.ap(), rs2bc)
        tk = self.v_ts(d.kqsq.ap(), d.kqsq.ap(), 1e-6, None, ALU.add, inc="cV")
        self.rsqrt_chain(d.rikq.ap(), d.kqsq.ap(), d.htmp32a.ap(), d.htmp32b.ap(), arg_tok=tk)
        self.v_tt(d.ckc.ap(), d.rikq_k.ap(), d.rsc.ap())
        self.v_tt(d.cqc.ap(), d.rikq_q.ap(), d.rsc.ap())

    def qCf_buf(self, h):
        return self.qCf if h % 2 == 0 else self.qCf2

    def p1C(self, s, h):
        d = self
        self.ctx = 'p1Ch%d' % h
        if self.g_last:
            self.wait("V", "cT", self.g_last)
        kAP = d.kvqf.ap3(0, [(192, 16), (1, 64)])
        vAP = d.kvqf.ap3(64, [(192, 16), (1, 64)])
        qAP = d.kvqf.ap3(128, [(192, 16), (1, 64)])
        qb = self.qCf_buf(h)
        kCf3 = d.keysCf.ap3(0, [(64, 16), (1, 64)])
        qCf3 = qb.ap3(0, [(64, 16), (1, 64)])
        self.v_tt(kCf3, kAP, d.ckc.ap3(0, [(1, 16), (0, 64)]))
        self.v_tt(d.keysCb.ap3(0, [(64, 16), (1, 64)]), kCf3,
                  bass.AP(d.knwf, 0, [[64, 128], [0, 16], [1, 64]]))
        self.v_tt(d.valsCb.ap3(0, [(64, 16), (1, 64)]), vAP, d.rsc.ap3(0, [(1, 16), (0, 64)]))
        self.v_tt(qCf3, qAP, d.cqc.ap3(0, [(1, 16), (0, 64)]))
        self.v_tt(qCf3, qCf3, bass.AP(d.qnwf, 0, [[64, 128], [0, 16], [1, 64]]))
        self.mat_tok = self.v_copy(d.qCb.ap(), qb.ap(), inc="cV")
        if s == 0 and h == 0:
            self.dump("keysCf", d.keysCf.ap(), [128, 1024])
            self.dump("qCf", qb.ap(), [128, 1024])
            self.dump("valsCb", d.valsCb.ap(), [128, 1024])

    def transpose_halves(self, halves, pn=64):
        d = self
        tok = None
        for (aps, dst_ap) in halves:
            hf = self.bt_half
            self.bt_half ^= 1
            if self.bt_free_h[hf]:
                self.wait("T", "cS", self.bt_free_h[hf])
            buf = d.BT0 if hf == 0 else d.BT1
            def trs(e, aps=aps, pn=pn, buf=buf):
                ins = None
                for i, in_ap in enumerate(aps):
                    ins = e.transpose(buf[0:pn, i * 128:(i + 1) * 128], in_ap, d.identb[:])
                return ins
            tk = self.op("T", trs, inc="cT")
            self.wait("S", "cT", tk)
            tok = self.s_copy(dst_ap, buf[0:pn, 0:512])
            self.bt_free_h[hf] = tok
        self.bt_free = tok
        return tok

    def p1D(self, s, h):
        d = self
        self.ctx = 'p1Dh%d' % h
        self.wait("T", "cV", self.mat_tok)
        for (srcb, dstb) in ((d.keysCb, d.keysT), (d.qCb, d.qT)):
            for g in range(2):
                halves = [([srcb.ap((g * 8 + 4 * k + j) * 64, 64) for j in range(4)],
                           dstb.ap(g * 1024 + k * 512, 512, pn=64)) for k in range(2)]
                self.transpose_halves(halves, pn=64)
            if dstb is d.keysT:
                self.keysT_ready = self.bt_free
        if s == 0 and h == 0:
            self.dump("keysT", d.keysT.ap(), [128, 2048])

    def p1EF(self, s, h):
        d = self
        self.ctx = 'p1EFh%d' % h
        B01 = [d.B0, d.B1]
        NP_ = T // 2
        f_toks = [0] * 8
        self.wait("T", "cS", self.keysT_ready)
        self.wtok("T", self.a_last)
        for u in range(8):
            bank = [d.B0, d.B1][u % 2]
            if u >= 2:
                self.wait("T", "cS", f_toks[u - 2])
            def p0T(e, u=u, bank=bank):
                return e.matmul(bank[:, 0:512],
                                d.w0b[0:64, (u % 2) * 128:(u % 2) * 128 + 128],
                                d.keysT.ap((u // 2) * 512, 512, pn=64), start=True, stop=True)
            mm = self.op("T", p0T, inc="cT")
            self.wait("S", "cT", mm)
            f_toks[u] = self.op("S", lambda e, u=u, bank=bank: e.activation(
                d.act1Tb.ap((u % 2) * 2048 + (u // 2) * 512, 512), bank[:, 0:512], AF.Gelu),
                inc="cS")
        self.act1T_ready = f_toks[7]
        self.f_toks = f_toks
        if s == 0 and h == 0:
            self.dump("act1Tb", d.act1Tb.ap(), [128, 4096])
        act_tok = [None] * NP_
        self.wait("T", "cS", f_toks[7])
        for p in range(NP_):
            if p >= 2:
                self.wait("T", act_tok[p - 2][0], act_tok[p - 2][1])
            def e_mm(e, p=p):
                e.matmul(B01[p % 2][:, 0:256], d.keysT.ap(2 * p * 128, 128, pn=64), d.w0b[:],
                         start=True, stop=True)
                return e.matmul(B01[p % 2][:, 256:512], d.keysT.ap((2 * p + 1) * 128, 128, pn=64),
                                d.w0b[:], start=True, stop=True)
            mm = self.op("T", e_mm, inc="cT")
            self.wait("S", "cT", mm)
            self.op("S", lambda e, p=p: e.activation(d.act1b.ap(p * 512, 512),
                                                     B01[p % 2][:, 0:512], AF.Gelu))
            act_tok[p] = self.emit_gg(B01[p % 2][:, 0:512], d.ggb.ap(p * 512, 512), mm)
        self.bank01_free = act_tok[NP_ - 1]
        if s == 0 and h == 0:
            self.dump("act1b", d.act1b.ap(), [128, 4096])
            self.dump("ggb", d.ggb.ap(), [128, 4096])

    def p1G(self, s, h):
        d = self
        self.ctx = 'p1Gh%d' % h
        B23 = [d.B2, d.B3]
        NG = T // 4
        p1_tok = [0] * NG
        if self.dh_tok:
            self.wait("S", "cV", self.dh_tok)
        if self.l_last:
            self.wtok("T", self.l_last)
        for g in range(NG):
            self.wait("T", "cS", self.f_toks[2 * g + 1])
            if g >= 2:
                self.wait("T", "cS", p1_tok[g - 2])
            def p1mm(e, g=g):
                ins = None
                for j in range(4):
                    t = 4 * g + j
                    e.matmul(B23[g % 2][:, j * 64:(j + 1) * 64], d.act1Tb.ap(t * 128, 128),
                             d.w1b[:, 0:64], start=True, stop=False)
                    ins = e.matmul(B23[g % 2][:, j * 64:(j + 1) * 64],
                                   d.act1Tb.ap(2048 + t * 128, 128),
                                   d.w1b[:, 64:128], start=False, stop=True)
                return ins
            mm = self.op("T", p1mm, inc="cT")
            self.wait("S", "cT", mm)
            p1_tok[g] = self.s_copy(d.pre1f.ap(g * 256, 256), B23[g % 2][:, 0:256])
        self.g_toks = ("cS", p1_tok[NG - 1])

    def p1H(self, s, h):
        d = self
        self.ctx = 'p1Hh%d' % h
        junk3 = d.ch2.ap3(0, [(64, 16), (1, 64)])
        p13 = d.pre1f.ap3(0, [(64, 16), (1, 64)])
        self.wtok("V", self.g_toks)
        self.v_tt(junk3, p13, p13)
        self.h_red_tok = self.op("V", lambda e: e.tensor_reduce(
            d.p1sq.ap3(0, [(1, 16), (0, 1)]), junk3, AX.X, ALU.add), inc="cV", drain=True)
        tk = self.v_ts(d.htmpa.ap(), d.p1sq.ap(), 1.0 / 64.0, 1e-8, ALU.mult, ALU.add, inc="cV")
        self.rsqrt_chain(d.rinv.ap(), d.htmpa.ap(), d.htmpb.ap(), d.p1sq.ap(), arg_tok=tk)
        if s == 0 and h == 0:
            self.dump("pre1f", d.pre1f.ap(), [128, 1024])
            self.dump("hcols", d.hcols[:], [128, 256])

    def p1I(self, s, h):
        d = self
        self.ctx = 'p1Ih%d' % h
        rinv_bc = d.rinv.ap3(0, [(1, 16), (0, 64)])
        pre1_3 = d.pre1f.ap3(0, [(64, 16), (1, 64)])
        hn3 = d.hnf.ap3(0, [(64, 16), (1, 64)])
        ch1_3 = d.ch1.ap3(0, [(64, 16), (1, 64)])
        ch2_3 = d.ch2.ap3(0, [(64, 16), (1, 64)])
        th2_2d = bass.AP(d.th2, h, [[256, 128], [16, 16]])
        self.v_tt(hn3, pre1_3, rinv_bc)
        self.v_tt(d.ch1.ap(), d.hnf.ap(), d.keysCb.ap(), ALU.add)
        self.v_tt(d.ch1.ap(), d.ch1.ap(), d.valsCb.ap(), ALU.subtract)
        self.v_tt(d.ch2.ap(), d.ch1.ap(), d.hnf.ap())
        self.op("V", lambda e: e.tensor_reduce(d.dyv.ap3(0, [(1, 16), (0, 1)]),
                                               ch2_3, AX.X, ALU.add), drain=True)
        self.v_tt(d.htmpa.ap(), th2_2d, d.rinv.ap())
        self.v_ts(d.dyv.ap(), d.dyv.ap(), 1.0 / 64.0, None, ALU.mult)
        self.v_tt(d.htmpb.ap(), d.dyv.ap(), d.htmpa.ap())
        self.v_tt(ch2_3, hn3, d.htmpb.ap3(0, [(1, 16), (0, 64)]))
        self.v_tt(ch1_3, ch1_3, d.htmpa.ap3(0, [(1, 16), (0, 64)]))
        self.dh_tok = self.v_tt(d.dhb.ap(), d.ch1.ap(), d.ch2.ap(), ALU.subtract, inc="cV")
        if s == 0 and h == 0:
            self.dump("dhb", d.dhb.ap(), [128, 1024])

    def p1J(self, s, h, a_cb=None):
        d = self
        self.ctx = 'p1Jh%d' % h
        B23 = [d.B2, d.B3]
        self.wait("T", "cV", self.dh_tok)
        for g in range(2):
            halves = [([d.dhb.ap((g * 8 + 4 * k + j) * 64, 64) for j in range(4)],
                       d.dhT.ap(g * 1024 + k * 512, 512, pn=64)) for k in range(2)]
            self.transpose_halves(halves, pn=64)
        self.wait("T", "cS", self.bt_free)
        if self.bank456_free:
            self.wait("T", "cV", self.bank456_free[0])
            self.wait("T", "cS", self.bank456_free[1])
        NP_ = T // 2
        mult_tok = [0] * NP_
        for p in range(NP_):
            if p >= 2:
                self.wait("T", "cV", mult_tok[p - 2])
            def dh0mm(e, p=p):
                e.matmul(B23[p % 2][:, 0:256], d.dhT.ap(2 * p * 128, 128, pn=64), d.w1tb[:],
                         start=True, stop=True)
                return e.matmul(B23[p % 2][:, 256:512], d.dhT.ap((2 * p + 1) * 128, 128, pn=64),
                                d.w1tb[:], start=True, stop=True)
            dh0 = self.op("T", dh0mm, inc="cT")
            def g1mm(e, p=p):
                ins = None
                for j in range(2):
                    t = 2 * p + j
                    e.matmul(d.B4[:, 0:64], d.act1b.ap(t * 256, 128), d.dhb.ap(t * 64, 64),
                             start=(t == 0), stop=(t == T - 1))
                    ins = e.matmul(d.B4[:, 64:128], d.act1b.ap(t * 256 + 128, 128),
                                   d.dhb.ap(t * 64, 64), start=False, stop=(t == T - 1))
                return ins
            self.op("T", g1mm)
            self.wait("V", "cT", dh0)
            mult_tok[p] = self.v_tt(d.dh0full.ap(p * 512, 512), B23[p % 2][:, 0:512],
                                    d.ggb.ap(p * 512, 512), inc="cV")
            if a_cb is not None:
                a_cb(p)
                self.ctx = 'p1Jh%d' % h
        self.wait("T", "cV", mult_tok[NP_ - 1])
        def g0mm(e):
            ins = None
            for t in range(T):
                ins = e.matmul(d.B5[0:64, 0:256], d.keysCb.ap(t * 64, 64),
                               d.dh0full.ap((t // 2) * 512 + (t % 2) * 256, 256),
                               start=(t == 0), stop=(t == T - 1))
            return ins
        g_last = self.op("T", g0mm, inc="cT")
        self.j_mult_last = (mult_tok[NP_ - 1], mult_tok[NP_ - 2])
        self.g_last = g_last

    def p1K(self, s, h):
        d = self
        self.ctx = 'p1Kh%d' % h
        self.wait("S", "cT", self.g_last)
        if self.b_red_tok:
            self.wait("S", "cV", self.b_red_tok)
        self.op("S", lambda e: e.activation(d.ch2.ap(0, 256, pn=64), d.B5[0:64, 0:256], AF.Square,
                                            accum_out=d.t11a.ap(0, 1, pn=64)))
        self.op("S", lambda e: e.activation(d.ch2.ap(0, 64), d.B4[:, 0:64], AF.Square,
                                            accum_out=d.t11b.ap(0, 1)))
        ssq_tok = self.op("S", lambda e: e.activation(d.ch2.ap(0, 64), d.B4[:, 64:128], AF.Square,
                                                      accum_out=d.t11c.ap(0, 1)), inc="cS")
        self.wait("T", "cS", ssq_tok)
        def ssqmm(e):
            e.matmul(d.B5[0:1, 480:481], d.onesf[0:64, 0:1], d.t11a.ap(0, 1, pn=64),
                     start=True, stop=False)
            e.matmul(d.B5[0:1, 480:481], d.onesf[:], d.t11b.ap(0, 1), start=False, stop=False)
            return e.matmul(d.B5[0:1, 480:481], d.onesf[:], d.t11c.ap(0, 1),
                            start=False, stop=True)
        mm = self.op("T", ssqmm, inc="cT")
        self.wait("S", "cT", mm)
        sq = self.op("S", lambda e: e.activation(d.t11a.ap(0, 1, pn=1), d.B5[0:1, 480:481],
                                                 AF.Sqrt), inc="cS")
        self.wait("V", "cS", sq)
        self.v_ts(d.t11a.ap(0, 1, pn=1), d.t11a.ap(0, 1, pn=1), 1e-6, None, ALU.add)
        self.op("V", lambda e: e.reciprocal(d.t11b.ap(0, 1, pn=1), d.t11a.ap(0, 1, pn=1)),
                drain=True)
        self.v_stt(d.t11c.ap(0, 1, pn=1), d.t11a.ap(0, 1, pn=1), -1.0, d.t11b.ap(0, 1, pn=1),
                   ALU.mult, ALU.mult)
        self.v_ts(d.t11c.ap(0, 1, pn=1), d.t11c.ap(0, 1, pn=1), 2.0, None, ALU.add)
        self.v_tt(d.t11b.ap(0, 1, pn=1), d.t11b.ap(0, 1, pn=1), d.t11c.ap(0, 1, pn=1))
        cf = self.v_ts(d.coef11.ap(0, 1, pn=1), d.t11b.ap(0, 1, pn=1), 10.0, 1.0,
                       ALU.mult, ALU.min, inc="cV")
        self.wait("T", "cV", cf)
        bc = self.op("T", lambda e: e.matmul(d.B5[:, 481:482], d.ones1[:],
                                             d.coef11.ap(0, 1, pn=1), start=True, stop=True),
                     inc="cT")
        self.wait("V", "cT", bc)
        self.v_copy(d.coefb.ap(0, 1), d.B5[:, 481:482])
        self.v_ts(d.negcoef.ap(0, 1), d.coefb.ap(0, 1), -1.0, None, ALU.mult)
        self.v_ts(d.onea.ap(0, 1), d.amh[:, h:h + 1], 1.0, -1.0, ALU.subtract, ALU.mult)
        self.v_ts(d.wtmp.ap(0, 256, pn=64), d.w0f[:], d.onea.ap(0, 1, pn=64), None, ALU.mult,
                  drain=False)
        self.v_stt(d.w0pb[:], d.B5[0:64, 0:256], d.negcoef.ap(0, 1, pn=64),
                   d.wtmp.ap(0, 256, pn=64), ALU.mult, ALU.add)
        self.v_ts(d.wtmp.ap(0, 128), d.w1f[:], d.onea.ap(0, 1), None, ALU.mult, drain=False)
        self.v_stt(d.w1pb[:, 0:64], d.B4[:, 0:64], d.negcoef.ap(0, 1),
                   d.wtmp.ap(0, 64), ALU.mult, ALU.add)
        wp_tok = self.v_stt(d.w1pb[:, 64:128], d.B4[:, 64:128], d.negcoef.ap(0, 1),
                            d.wtmp.ap(64, 64), ALU.mult, ALU.add, inc="cV")
        self.wp_tok = wp_tok
        self.ssq_tok = ssq_tok
        self.bank456_free = (wp_tok, ssq_tok)
        if s == 0 and h == 0:
            self.dump("w0pb", d.w0pb[:], [64, 256])
            self.dump("w1pb", d.w1pb[:], [128, 128])
            self.dump("coefb", d.coefb.ap(0, 1), [128, 1])

    def p1L(self, s, h):
        d = self
        self.ctx = 'p1Lh%d' % h
        B23 = [d.B2, d.B3]
        self.wait("T", "cV", self.wp_tok)
        self.wait("T", "cS", self.ssq_tok)
        f_toks = [0] * 8
        for u in range(8):
            bank = [d.B4, d.B5][u % 2]
            if u >= 2:
                self.wait("T", "cS", f_toks[u - 2])
            def p0pT(e, u=u, bank=bank):
                return e.matmul(bank[:, 0:512],
                                d.w0pb[0:64, (u % 2) * 128:(u % 2) * 128 + 128],
                                d.qT.ap((u // 2) * 512, 512, pn=64), start=True, stop=True)
            mm = self.op("T", p0pT, inc="cT")
            self.wait("S", "cT", mm)
            f_toks[u] = self.op("S", lambda e, u=u, bank=bank: e.activation(
                d.act1Tb.ap((u % 2) * 2048 + (u // 2) * 512, 512), bank[:, 0:512], AF.Gelu),
                inc="cS")
        NG = T // 4
        p1_tok = [0] * NG
        if self.n_at:
            self.wait("S", "cV", self.n_at)
        self.wait("T", "cS", f_toks[7])
        self.wait("T", "cV", self.j_mult_last[0])
        self.wait("T", "cV", self.j_mult_last[1])
        for g in range(NG):
            if g >= 2:
                self.wait("T", "cS", p1_tok[g - 2])
            def p1pmm(e, g=g):
                ins = None
                for j in range(4):
                    t = 4 * g + j
                    e.matmul(B23[g % 2][:, j * 64:(j + 1) * 64], d.act1Tb.ap(t * 128, 128),
                             d.w1pb[:, 0:64], start=True, stop=False)
                    ins = e.matmul(B23[g % 2][:, j * 64:(j + 1) * 64],
                                   d.act1Tb.ap(2048 + t * 128, 128),
                                   d.w1pb[:, 64:128], start=False, stop=True)
                return ins
            mm = self.op("T", p1pmm, inc="cT")
            self.wait("S", "cT", mm)
            p1_tok[g] = self.s_copy(d.pre1pf.ap(g * 256, 256), B23[g % 2][:, 0:256])
        self.l_last = ("cS", p1_tok[NG - 1])

    def p1M(self, s, h):
        d = self
        self.ctx = 'p1Mh%d' % h
        junk3 = d.ch2.ap3(0, [(64, 16), (1, 64)])
        pp3 = d.pre1pf.ap3(0, [(64, 16), (1, 64)])
        if self.dh_tok:
            self.wait("S", "cV", self.dh_tok)
        self.wtok("S", self.l_last)
        msq = self.op("S", lambda e: e.activation(junk3, pp3, AF.Square), inc="cS")
        self.wait("V", "cS", msq)
        self.op("V", lambda e: e.tensor_reduce(d.p1psq.ap3(0, [(1, 16), (0, 1)]), junk3,
                                               AX.X, ALU.add), drain=True)
        tk = self.v_ts(d.htmpc.ap(), d.p1psq.ap(), 1.0 / 64.0, 1e-8, ALU.mult, ALU.add, inc="cV")
        self.rsqrt_chain(d.rinvp.ap(), d.htmpc.ap(), d.htmpd.ap(), d.p1psq.ap(), arg_tok=tk)

    def p1N(self, s, h):
        d = self
        self.ctx = 'p1Nh%d' % h
        if self.attn_tok:
            self.wait("V", "sAT", self.attn_tok)
        qb = self.qCf_buf(h)
        self.v_tt(d.attnf.ap3(0, [(64, 16), (1, 64)]), d.pre1pf.ap3(0, [(64, 16), (1, 64)]),
                  d.rinvp.ap3(0, [(1, 16), (0, 64)]))
        at = self.n_at = self.v_tt(d.attnf.ap(), d.attnf.ap(), qb.ap(), ALU.add, inc="cV")
        if s == 0 and h == 0:
            self.dump("attnf", d.attnf.ap(), [128, 1024])
        self.wait("Q", "cV", at)
        dst = bass.AP(self.d_out, s * S * D + h * 64, [[D, 128], [128 * D, 16], [1, 64]])
        self.attn_tok = self.dma("Q", lambda e, dst=dst: e.dma_start(
            out=dst, in_=d.attnf.ap3(0, [(64, 16), (1, 64)])), "sAT")

    def emit_gg(self, psum_ap, out_ap, mm_tok):
        d = self
        if not SIM_FP64:
            tok = self.op("S", lambda e: e.activation(out_ap, psum_ap, AF.Derivative_Gelu),
                          inc="cS")
            return ("cS", tok)
        W = 256
        gel = d.ch1.ap(0, W)
        e1 = d.hnf.ap(0, W)
        e2 = d.pre1f.ap(0, W)
        m = d.outt.ap(0, W)
        tok = self.op("S", lambda e: e.activation(gel, psum_ap, AF.Gelu), inc="cS")
        self.wait("V", "cS", tok)
        self.v_ts(m, psum_ap, 0.0, None, ALU.abs_max)
        self.v_ts(m, m, 1e-30, None, ALU.is_ge)
        self.v_ts(e1, m, 1.0, None, ALU.subtract)
        self.v_tt(e2, psum_ap, e1, ALU.subtract)
        self.v_tt(gel, gel, e2, ALU.divide)
        self.v_tt(gel, gel, m)
        self.v_ts(e1, e1, -0.5, None, ALU.mult)
        self.v_tt(gel, gel, e1, ALU.add)
        self.v_tt(e1, psum_ap, psum_ap)
        tok = self.v_copy(e1, e1, inc="cV")
        self.wait("S", "cV", tok)
        tok = self.op("S", lambda e: e.activation(e2, e1, AF.Exp, scale=-0.5), inc="cS")
        self.wait("V", "cS", tok)
        self.v_tt(e2, e2, psum_ap)
        chi = float(np.float32(0.3989422804014327))
        clo = float(np.float64(0.3989422804014327) - np.float64(np.float32(0.3989422804014327)))
        self.v_ts(e1, e2, chi, None, ALU.mult)
        self.v_tt(gel, gel, e1, ALU.add)
        self.v_ts(e1, e2, clo, None, ALU.mult)
        tok = self.v_tt(out_ap, gel, e1, ALU.add, inc="cV")
        return ("cV", tok)

    def p2(self, s):
        d = self
        self.ctx = 'p2s%d' % s
        xt = [d.xtile.ap(0, 1024), d.xtile.ap(1024, 1024)]
        self.tbar()
        self.wait("Q", "cV", self.n["cV"])
        attn_done = self.n["sAT"]
        xc_tok = [0] * T
        xb_tok = [0] * T
        ev_tok = [0] * T
        ffn_tok = [0] * T
        acc_tok = [0] * T
        cur_tok = [0] * T
        cur_used = [0] * T
        for t in range(T):
            if t >= 2:
                self.wait("Q", "cV", xc_tok[t - 2])
                self.wait("Q", "cV", xb_tok[t - 2])
            src = bass.AP(d.d_x, s * S * D + t * 128 * D, [[D, 128], [1, 1024]])
            sem = "sX%d" % (t % 2)
            ld = self.dma("Q", lambda e, src=src, t=t: e.dma_start(out=xt[t % 2], in_=src), sem)
            self.wait("V", sem, ld)
            xc_tok[t] = self.v_ts(d.ch1.ap(), xt[t % 2], d.mcol.ap(t, 1), None,
                                  ALU.subtract, inc="cV")
            a = d.htmpa.ap(0, 1)
            b = d.htmpb.ap(0, 1)
            r = d.htmpa.ap(1, 1)
            cmb = d.htmpb.ap(1, 1)
            self.op("V", lambda e: e.reduce_max(a, d.ch1.ap(), AX.X,
                                                apply_absolute_value=True), drain=True)
            self.v_tt(b, a, d.rstd.ap(t, 1))
            self.v_ts(b, b, 1e-6, None, ALU.max)
            self.op("V", lambda e: e.reciprocal(r, b), drain=True)
            self.v_stt(cmb, b, -1.0, r, ALU.mult, ALU.mult)
            self.v_ts(cmb, cmb, 2.0, None, ALU.add)
            self.v_tt(r, r, cmb)
            self.v_tt(d.xscl.ap(t, 1), b, d.c127[:])
            self.v_ts(r, r, 127.0, None, ALU.mult)
            self.v_tt(cmb, r, d.rstd.ap(t, 1))
            self.v_ts(d.ch2.ap(), d.ch1.ap(), cmb, None, ALU.mult)
            self.v_ts(d.hnf.ap(), d.ch2.ap(), M32, -M32, ALU.add, ALU.add)
            self.v_ts(d.pre1f.ap(), d.ch2.ap(), M52, -M52, ALU.add, ALU.add)
            self.v_ts(d.pre1f.ap(), d.pre1f.ap(), d.gflag[:], None, ALU.mult)
            q_tok = self.v_stt(d.xqbt.ap(), d.hnf.ap(), d.gm1f[:], d.pre1f.ap(),
                               ALU.mult, ALU.add, inc="cV")
            self.wait("T", "cV", q_tok)
            halves = [([d.xqbt.ap((4 * k + j) * 128, 128) for j in range(4)],
                       d.xqTt.ap((t % 2) * 1024 + k * 512, 512)) for k in range(2)]
            ev_tok[t] = self.transpose_halves(halves, pn=128)
            self.wait("T", "cS", ev_tok[t])
            if t >= 1:
                self.wait("T", "cV", ffn_tok[t - 1])
            def ffnmm(e, t=t):
                ins = None
                for j in range(8):
                    lhsT = d.xqTt.ap((t % 2) * 1024 + j * 128, 128)
                    e.matmul(d.B0[:, 0:512], lhsT, d.wqtb[:, j * 1024: j * 1024 + 512],
                             start=(j == 0), stop=(j == 7))
                    ins = e.matmul(d.B1[:, 0:512], lhsT,
                                   d.wqtb[:, j * 1024 + 512: (j + 1) * 1024],
                                   start=(j == 0), stop=(j == 7))
                return ins
            mm = self.op("T", ffnmm, inc="cT")
            if t == 0:
                self.wait("Q", "sAT", attn_done)
            if t >= 2:
                self.wait("Q", "cV", cur_used[t - 2])
            srcc = bass.AP(self.d_out, s * S * D + t * 128 * D, [[D, 128], [1, 1024]])
            cur_tok[t] = self.dma("Q", lambda e, srcc=srcc, t=t: e.dma_start(
                out=d.curb.ap((t % 2) * 1024, 1024), in_=srcc), "sC%d" % (t % 2))
            self.wait("V", "cT", mm)
            if t >= 2:
                self.wait("V", "sA%d" % (t % 2), acc_tok[t - 2])
            xb_tok[t] = self.v_tt(d.ch2.ap(), xt[t % 2], d.biasf[:], ALU.add, inc="cV")
            xscol = d.xscl.ap(t, 1)
            self.v_tt(d.hnf.ap(0, 512), d.B0[:, 0:512], d.wscf[:, 0:512])
            self.v_stt(d.outt.ap((t % 2) * 1024, 512), d.hnf.ap(0, 512), xscol,
                       d.ch2.ap(0, 512), ALU.mult, ALU.add)
            self.v_tt(d.hnf.ap(512, 512), d.B1[:, 0:512], d.wscf[:, 512:1024])
            self.v_stt(d.outt.ap((t % 2) * 1024 + 512, 512), d.hnf.ap(512, 512),
                       xscol, d.ch2.ap(512, 512), ALU.mult, ALU.add)
            self.wait("V", "sC%d" % (t % 2), cur_tok[t])
            ffn_tok[t] = cur_used[t] = self.v_tt(
                d.outt.ap((t % 2) * 1024, 1024), d.outt.ap((t % 2) * 1024, 1024),
                d.curb.ap((t % 2) * 1024, 1024), ALU.add, inc="cV")
            self.wait("P", "cV", ffn_tok[t])
            dstp = bass.AP(self.d_out, s * S * D + t * 128 * D, [[D, 128], [1, 1024]])
            acc_tok[t] = self.dma("P", lambda e, dstp=dstp, t=t: e.dma_start(
                out=dstp, in_=d.outt.ap((t % 2) * 1024, 1024)), "sA%d" % (t % 2))
        if s == 0:
            self.dump("xqbt15", d.xqbt.ap(), [128, 1024])
            self.dump("outt", d.outt.ap(), [128, 2048])


def _build_and_run(shards, host):
    mk = MK()
    nc = mk.build()
    in_maps = []
    for xc in shards:
        m = dict(host)
        m["x"] = xc
        in_maps.append(m)
    res = run_bass_kernel_spmd(nc, in_maps, core_ids=list(range(len(shards))))
    return np.concatenate([r["out"] for r in res.results], axis=0)


def kernel(L0_attn_x, L0_attn_W_K, L0_attn_W_V, L0_attn_W_Q, L0_attn_mem_W0, L0_attn_mem_W1,
           L0_attn_k_norm_w, L0_attn_q_norm_w, L0_attn_store_norm_w, L0_attn_retrieve_norm_w,
           L0_attn_alpha_w, L0_attn_alpha_b, L0_attn_theta_w, L0_attn_theta_b,
           L0_attn_eta_w, L0_attn_eta_b, L0_ffn_weight_q, L0_ffn_weight_scale, L0_ffn_bias):
    f32 = lambda a: np.ascontiguousarray(np.asarray(a), dtype=np.float32)
    x = f32(L0_attn_x)
    wk, wv, wq = f32(L0_attn_W_K), f32(L0_attn_W_V), f32(L0_attn_W_Q)
    wcat = np.empty((D, H, 192), np.float32)
    wcat[:, :, 0:64] = wk.reshape(D, H, 64)
    wcat[:, :, 64:128] = wv.reshape(D, H, 64)
    wcat[:, :, 128:192] = wq.reshape(D, H, 64)
    wcat = np.ascontiguousarray(wcat.reshape(D, H * 192))
    wab = np.ascontiguousarray(np.concatenate(
        [f32(L0_attn_alpha_w), f32(L0_attn_theta_w), f32(L0_attn_eta_w)], axis=1))
    ab = np.ascontiguousarray(np.concatenate(
        [f32(L0_attn_alpha_b), f32(L0_attn_theta_b), f32(L0_attn_eta_b)]))
    w1 = f32(L0_attn_mem_W1)
    host = {
        "wcat": wcat, "wab": wab, "ab": ab,
        "w0": f32(L0_attn_mem_W0), "w1": w1,
        "w1t": np.ascontiguousarray(w1.T),
        "knw": f32(L0_attn_k_norm_w), "qnw": f32(L0_attn_q_norm_w),
        "snw": f32(L0_attn_store_norm_w), "rnw": f32(L0_attn_retrieve_norm_w),
        "wqt8": np.ascontiguousarray(np.asarray(L0_ffn_weight_q).T),
        "wsc": f32(L0_ffn_weight_scale), "fb": f32(L0_ffn_bias),
        "ident": np.eye(128, dtype=np.float32).astype(ml_dtypes.bfloat16),
    }
    shards = [np.ascontiguousarray(x[c * BPC:(c + 1) * BPC]) for c in range(NCORE)]
    out = _build_and_run(shards, host)
    return out


# version 39 (final)
# speedup vs baseline: 1.0601x; 1.0172x over previous
import os
import numpy as np
import ml_dtypes
import concourse.bass as bass
import concourse.mybir as mybir
from concourse.bass_utils import run_bass_kernel_spmd

F32 = mybir.dt.float32
BF16 = mybir.dt.bfloat16
I8 = mybir.dt.int8
AF = mybir.ActivationFunctionType
ALU = mybir.AluOpType
AX = mybir.AxisListType

M32 = 12582912.0
M52 = 6755399441055744.0

NCORE = 8
BPC = 2
S, D, H, HD, HID = 2048, 1024, 16, 64, 256
T = S // 128

SIM_FP64 = os.environ.get("BASS_SIM_FP64") == "1"


class Buf:

    def __init__(self, t, W, off, w):
        self.t, self.W, self.off, self.w = t, W, off, w

    def ap(self, c0=0, cw=None, p0=0, pn=128):
        cw = self.w - c0 if cw is None else cw
        return bass.AP(self.t, p0 * self.W + self.off + c0, [[self.W, pn], [1, cw]])

    def ap3(self, c0, pairs, p0=0, pn=128):
        return bass.AP(self.t, p0 * self.W + self.off + c0,
                       [[self.W, pn]] + [[st, c] for (st, c) in pairs])


class Arena:
    def __init__(self, t, W):
        self.t, self.W, self.off = t, W, 0

    def alloc(self, w):
        assert self.off + w <= self.W, f"arena overflow: {self.off}+{w} > {self.W}"
        b = Buf(self.t, self.W, self.off, w)
        self.off += w
        return b

    def overlay(self, at, w, extra=0):
        assert at.off + extra + w <= self.W
        return Buf(self.t, self.W, at.off + extra, w)


DEBUG = False


class MK:
    def __init__(self, debug=False):
        self.nc = bass.Bass("TRN2", target_bir_lowering=False)
        self.cmds = {k: [] for k in "QPTVS"}
        self.meta = {k: [] for k in "QPTVS"}
        self.n = {}
        self.sems = {}
        self.n_instr = 0
        self.attn_tok = 0
        self.bt_free = 0
        self.bt_free_h = [0, 0]
        self.bt_half = 0
        self.l_gelu_tok = 0
        self.n_at = 0
        self.mat_tok = 0
        self.dh_tok = 0
        self.b_red_tok = 0
        self.h_red_tok = 0
        self.keysT_ready = 0
        self.debug = debug or DEBUG
        self.dump_names = []

    def dump(self, name, src_ap, shape):
        if not self.debug:
            return
        dt = nc_dt = F32
        dout = self.nc.dram_tensor("dbg_" + name, list(shape), nc_dt, kind="ExternalOutput")
        self.dump_names.append("dbg_" + name)
        self.wait("P", "cV", self.n["cV"])
        self.wait("P", "cS", self.n["cS"])
        self.wait("P", "cT", self.n["cT"])
        tok = self.dma("P", lambda e: e.dma_start(
            out=bass.AP(dout, 0, [[shape[1], shape[0]], [1, shape[1]]]), in_=src_ap), "sDB")
        for eng in "TVSQ":
            self.wait(eng, "sDB", tok)

    def wait(self, eng, sem, val):
        if val <= 0:
            return
        h = self.sems[sem]
        self.cmds[eng].append(lambda e, h=h, v=val: e.wait_ge(h, v))
        self.meta[eng].append(("w", sem, val, getattr(self, "ctx", "?")))

    def op(self, eng, fn, inc=None, step=1, drain=False):
        tok = None
        if inc is not None:
            self.n[inc] += step
            tok = self.n[inc]
            h = self.sems[inc]
            self.cmds[eng].append(lambda e, fn=fn, h=h, st=step: fn(e).then_inc(h, st))
        else:
            self.cmds[eng].append(lambda e, fn=fn: fn(e))
        self.meta[eng].append(("i", inc, step if inc else 0, getattr(self, "ctx", "?")))
        self.n_instr += 1
        if eng in "VS" and (drain or SIM_FP64):
            self.cmds[eng].append(lambda e: e.drain())
            self.meta[eng].append(("i", None, 0, "drain"))
        return tok

    def dma(self, eng, fn, sem):
        return self.op(eng, fn, inc=sem, step=16)

    def tbar(self):
        self.wait("T", "cV", self.n["cV"])
        self.wait("T", "cS", self.n["cS"])

    def deadlock_check(self):
        state = {k: 0 for k in self.n}
        ptr = {e: 0 for e in self.cmds}
        progress = True
        while progress:
            progress = False
            for e in self.cmds:
                while ptr[e] < len(self.meta[e]):
                    m = self.meta[e][ptr[e]]
                    if m[0] == "w":
                        if state.get(m[1], 0) < m[2]:
                            break
                    elif m[1] is not None:
                        state[m[1]] = state.get(m[1], 0) + m[2]
                    ptr[e] += 1
                    progress = True
        stuck = {e: (ptr[e], len(self.meta[e])) for e in self.cmds if ptr[e] < len(self.meta[e])}
        detail = {e: self.meta[e][ptr[e]] for e in stuck}
        assert not stuck, f"DEADLOCK in schedule: {stuck}; waits={detail}; sems={state}"

    def build(self):
        nc = self.nc
        self.d_x = nc.dram_tensor("x", [BPC, S, D], F32, kind="ExternalInput")
        self.d_wcat = nc.dram_tensor("wcat", [D, H * 192], F32, kind="ExternalInput")
        self.d_wab = nc.dram_tensor("wab", [D, 48], F32, kind="ExternalInput")
        self.d_ab = nc.dram_tensor("ab", [48], F32, kind="ExternalInput")
        self.d_w0 = nc.dram_tensor("w0", [HD, HID], F32, kind="ExternalInput")
        self.d_w1 = nc.dram_tensor("w1", [HID, HD], F32, kind="ExternalInput")
        self.d_w1t = nc.dram_tensor("w1t", [HD, HID], F32, kind="ExternalInput")
        self.d_knw = nc.dram_tensor("knw", [HD], F32, kind="ExternalInput")
        self.d_qnw = nc.dram_tensor("qnw", [HD], F32, kind="ExternalInput")
        self.d_snw = nc.dram_tensor("snw", [D], F32, kind="ExternalInput")
        self.d_rnw = nc.dram_tensor("rnw", [D], F32, kind="ExternalInput")
        self.d_wqt = nc.dram_tensor("wqt8", [D, D], I8, kind="ExternalInput")
        self.d_wsc = nc.dram_tensor("wsc", [D], F32, kind="ExternalInput")
        self.d_fb = nc.dram_tensor("fb", [D], F32, kind="ExternalInput")
        self.d_ident = nc.dram_tensor("ident", [128, 128], BF16, kind="ExternalInput")
        self.d_out = nc.dram_tensor("out", [BPC, S, D], F32, kind="ExternalOutput")

        import contextlib
        with contextlib.ExitStack() as stk:
            self._alloc(stk)
            self._emit()
            self.deadlock_check()
            blk = stk.enter_context(nc.Block())

            @blk.sync
            def _(e):
                for fn in self.cmds["Q"]:
                    fn(e)

            @blk.gpsimd
            def _(e):
                for fn in self.cmds["P"]:
                    fn(e)

            @blk.tensor
            def _(e):
                for fn in self.cmds["T"]:
                    fn(e)

            @blk.vector
            def _(e):
                for fn in self.cmds["V"]:
                    fn(e)

            @blk.scalar
            def _(e):
                for fn in self.cmds["S"]:
                    fn(e)

        return nc

    def _alloc(self, stk):
        nc = self.nc
        sb = lambda n, s, d: stk.enter_context(nc.sbuf_tensor(n, s, d))
        ps = lambda n, s, d: stk.enter_context(nc.psum_tensor(n, s, d))
        for name in ["sK", "sW0", "sW1", "sX0", "sX1", "sAT", "sPW", "sA0", "sA1",
                     "sC0", "sC1", "sDB", "cT", "cV", "cS"]:
            self.sems[name] = stk.enter_context(nc.semaphore(name))
            self.n[name] = 0

        self.wcatb = sb("wcatb", [128, 8 * 3072], BF16)
        self.wqtb = sb("wqtb", [128, 8 * 1024], BF16)
        self.wabb = sb("wabb", [128, 8 * 48], BF16)
        self.w0b = sb("w0b", [64, 256], BF16)
        self.w1b = sb("w1b", [128, 128], BF16)
        self.w1tb = sb("w1tb", [64, 256], BF16)
        self.w0f = sb("w0f", [64, 256], F32)
        self.w1f = sb("w1f", [128, 128], F32)
        self.identb = sb("identb", [128, 128], BF16)
        self.knwf = sb("knwf", [128, 64], F32)
        self.qnwf = sb("qnwf", [128, 64], F32)
        self.abf = sb("abf", [128, 48], F32)
        self.wscf = sb("wscf", [128, 1024], F32)
        self.biasf = sb("biasf", [128, 1024], F32)
        self.snwc = sb("snwc", [128, 8], F32)
        self.rnwc = sb("rnwc", [128, 8], F32)
        self.onesf = sb("onesf", [128, 1], F32)
        self.ones1 = sb("ones1", [1, 128], F32)
        self.gflag = sb("gflag", [128, 1], F32)
        self.gm1f = sb("gm1f", [128, 1], F32)
        self.c127 = sb("c127", [128, 1], F32)
        self.ams = sb("ams", [1, 32], F32)
        self.w0pb = sb("w0pb", [64, 256], BF16)
        self.w1pb = sb("w1pb", [128, 128], BF16)

        self.xnT = sb("xnT", [128, 8 * 2048], BF16)
        self.th2 = sb("th2", [128, 256], F32)
        self.amh = sb("amh", [128, 32], F32)
        self.scols = sb("scols", [128, 16 * 10], F32)
        sc = lambda i: Buf(self.scols, 16 * 10, i * 16, 16)
        (self.negsum, self.ssqc, self.mcol, self.nmcol, self.rstd,
         self.rsc, self.rs2, self.varc, self.xscl, self.tmpP0) = [sc(i) for i in range(10)]

        HCW = 16 * 20
        self.hcols = sb("hcols", [128, HCW], F32)
        hc = lambda i: Buf(self.hcols, HCW, i * 16, 16)
        (self.kqsq_k, self.kqsq_q, self.rikq_k, self.rikq_q, self.ckc, self.cqc,
         self.p1sq, self.rinv, self.dyv, self.rinvp, self.htmpa, self.htmpb,
         self._h12, self._h13, self._h14, self._h15,
         self.p1psq, self.htmpc, self.htmpd, self._h19) = [hc(i) for i in range(20)]
        self.kqsq = Buf(self.hcols, HCW, 0, 32)
        self.rikq = Buf(self.hcols, HCW, 2 * 16, 32)
        self.htmp32a = Buf(self.hcols, HCW, 12 * 16, 32)
        self.htmp32b = Buf(self.hcols, HCW, 14 * 16, 32)
        self.tiny = sb("tiny", [128, 8], F32)
        tn = lambda i: Buf(self.tiny, 8, i, 1)
        (self.ssq11, self.coef11, self.coefb, self.negcoef, self.onea,
         self.t11a, self.t11b, self.t11c) = [tn(i) for i in range(8)]

        AFW = 13360 + 96
        self.arF_t = sb("arF", [128, AFW], F32)
        arF = Arena(self.arF_t, AFW)
        self.kvqf = arF.alloc(3072)
        self.keysCf = arF.alloc(1024)
        self.qCf = arF.alloc(1024)
        self.pre1f = arF.alloc(1024)
        self.hnf = arF.alloc(1024)
        self.ch1 = arF.alloc(1024)
        self.ch2 = arF.alloc(1024)
        self.xtile = arF.alloc(2048)
        self.outt = arF.alloc(2048)
        self.sgin = arF.alloc(48)
        self.sgout = arF.alloc(48)
        self.wstg = arF.overlay(self.kvqf, 2 * 3072)
        self.attnf = arF.overlay(self.xtile, 1024)
        self.qCf2 = arF.overlay(self.xtile, 1024, extra=1024)
        self.curb = arF.overlay(self.kvqf, 2048, extra=1024)
        self.pre1pf = arF.overlay(self.outt, 1024)
        self.wtmp = arF.overlay(self.outt, 256, extra=1024)

        ABW = 20992
        self.arB_t = sb("arB", [128, ABW], BF16)
        arB = Arena(self.arB_t, ABW)
        self.keysCb = arB.alloc(1024)
        self.valsCb = arB.alloc(1024)
        self.qCb = arB.alloc(1024)
        self.keysT = arB.alloc(2048)
        self.qT = arB.alloc(2048)
        self.act1b = arB.alloc(4096)
        self.act1Tb = arB.alloc(4096)
        self.ggb = arB.alloc(4096)
        self.dhb = arB.alloc(1024)
        self.dh0b = arB.alloc(512)
        self.dh0full = arB.overlay(self.act1Tb, 4096)
        self.dhT = arB.overlay(self.keysT, 2048)
        self.xnb = arB.overlay(self.dhb, 1024)
        self.xqbt = arB.overlay(self.dhb, 1024)
        self.xqTt = arB.overlay(self.qT, 2048)

        self.B0 = ps("B0", [128, 512], F32)
        self.B1 = ps("B1", [128, 512], F32)
        self.B2 = ps("B2", [128, 512], F32)
        self.B3 = ps("B3", [128, 512], F32)
        self.B4 = ps("B4", [128, 512], F32)
        self.B5 = ps("B5", [128, 512], F32)
        self.BT1 = ps("BT1", [128, 1024], BF16)
        self.BT0 = ps("BT0", [128, 1024], BF16)

    @staticmethod
    def _tiny_out(out):
        try:
            pairs = list(out.ap)
            n = 1
            for p in pairs[1:]:
                n *= int(p[1])
            return n < 256
        except Exception:
            return True

    def v_ts(self, out, in0, s1, s2=None, op0=ALU.mult, op1=ALU.bypass, inc=None, drain=None):
        drain = self._tiny_out(out) if drain is None else drain
        return self.op("V", lambda e: e.tensor_scalar(out, in0, s1, s2, op0, op1),
                       inc=inc, drain=drain)

    def v_tt(self, out, in0, in1, alu=ALU.mult, inc=None, drain=None):
        drain = self._tiny_out(out) if drain is None else drain
        return self.op("V", lambda e: e.tensor_tensor(out, in0, in1, alu), inc=inc, drain=drain)

    def v_stt(self, out, in0, s, in1, op0, op1, inc=None, drain=None):
        drain = self._tiny_out(out) if drain is None else drain
        return self.op("V", lambda e: e.scalar_tensor_tensor(out, in0, s, in1, op0, op1),
                       inc=inc, drain=drain)

    def v_copy(self, out, in_, inc=None, drain=None):
        drain = self._tiny_out(out) if drain is None else drain
        return self.op("V", lambda e: e.tensor_copy(out, in_), inc=inc, drain=drain)

    def s_copy(self, out, in_, inc="cS"):
        return self.op("S", lambda e: e.activation(out, in_, AF.Copy), inc=inc)

    def wtok(self, eng, tok):
        if tok:
            self.wait(eng, tok[0], tok[1])

    def newton_rsqrt(self, y, a, tmp, iters=1):
        for _ in range(iters):
            self.v_tt(tmp, y, y)
            self.v_tt(tmp, tmp, a)
            self.v_ts(tmp, tmp, -0.5, 1.5, ALU.mult, ALU.add)
            self.v_tt(y, y, tmp)

    def rsqrt_chain(self, dst, arg, tmp1, tmp2, arg_tok=None):
        self.wait("S", "cV", arg_tok if arg_tok else self.n["cV"])
        stok = self.op("S", lambda e: e.activation(tmp1, arg, AF.Sqrt), inc="cS")
        self.wait("V", "cS", stok)
        self.op("V", lambda e: e.reciprocal(dst, tmp1), drain=True)
        self.newton_rsqrt(dst, arg, tmp2)

    def _emit(self):
        self.prep()
        for s in range(BPC):
            self.p0(s)
            self.bank01_free = None
            self.bank456_free = None
            self.l_last = None
            self.g_last = 0
            self.p1A(s, 0)
            self.p1B(s, 0)
            self.p1C(s, 0)
            self.p1D(s, 0)
            self.p1EF(s, 0)
            for h in range(H):
                self.p1G(s, h)
                self.p1H(s, h)
                if h + 1 < H:
                    self.p1A(s, h + 1)
                self.p1I(s, h)
                self.p1J(s, h)
                self.p1K(s, h)
                if h >= 1:
                    self.p1M(s, h - 1)
                    self.p1N(s, h - 1)
                if h + 1 < H:
                    self.p1B(s, h + 1)
                    self.p1C(s, h + 1)
                self.p1L(s, h)
                if h + 1 < H:
                    self.p1D(s, h + 1)
                    self.p1EF(s, h + 1)
            self.p1M(s, H - 1)
            self.p1N(s, H - 1)
            self.p2(s)
        for sem in ["sPW", "sA0", "sA1", "sDB"]:
            self.wait("P", sem, self.n[sem])
        for sem in ["sK", "sW0", "sW1", "sX0", "sX1", "sAT", "sC0", "sC1"]:
            self.wait("Q", sem, self.n[sem])

    def prep(self):
        d = self
        self.ctx = 'prep' 
        snw_src = bass.AP(d.d_snw, 0, [[1, 128], [128, 8]])
        rnw_src = bass.AP(d.d_rnw, 0, [[1, 128], [128, 8]])
        knw_src = bass.AP(d.d_knw, 0, [[0, 128], [1, 64]])
        qnw_src = bass.AP(d.d_qnw, 0, [[0, 128], [1, 64]])
        ab_src = bass.AP(d.d_ab, 0, [[0, 128], [1, 48]])
        wsc_src = bass.AP(d.d_wsc, 0, [[0, 128], [1, 1024]])
        fb_src = bass.AP(d.d_fb, 0, [[0, 128], [1, 1024]])
        w0_src = bass.AP(d.d_w0, 0, [[256, 64], [1, 256]])
        w1_src = bass.AP(d.d_w1, 0, [[64, 128], [64 * 128, 2], [1, 64]])
        w1t_src = bass.AP(d.d_w1t, 0, [[256, 64], [1, 256]])
        id_src = bass.AP(d.d_ident, 0, [[128, 128], [1, 128]])

        for dst, src in [(d.snwc[:], snw_src), (d.rnwc[:], rnw_src), (d.knwf[:], knw_src),
                         (d.qnwf[:], qnw_src), (d.abf[:], ab_src), (d.wscf[:], wsc_src),
                         (d.biasf[:], fb_src), (d.w0f[:], w0_src),
                         (bass.AP(d.w1f, 0, [[128, 128], [64, 2], [1, 64]]), w1_src),
                         (d.identb[:], id_src)]:
            self.dma("Q", lambda e, dst=dst, src=src: e.dma_start(
                out=dst, in_=src, allow_slow_non_contiguous=True), "sK")
        q_small = self.n["sK"]
        for dst, src in [(d.w0b[:], w0_src),
                         (bass.AP(d.w1b, 0, [[128, 128], [64, 2], [1, 64]]), w1_src),
                         (d.w1tb[:], w1t_src)]:
            self.dma("P", lambda e, dst=dst, src=src: e.dma_start(out=dst, in_=src), "sPW")
        for i in range(8):
            src = bass.AP(d.d_wqt, i * 128 * 1024, [[1024, 128], [1, 1024]])
            self.dma("P", lambda e, src=src, i=i: e.dma_start(
                out=d.wqtb[:, i * 1024:(i + 1) * 1024], in_=src), "sPW")
        self.prep_done_p = self.n["sPW"]

        self.op("V", lambda e: e.memset(d.onesf[:], 1.0))
        self.op("V", lambda e: e.memset(d.ones1[:], 1.0))
        self.op("V", lambda e: e.memset(d.gflag[:], 1.0))
        self.op("V", lambda e: e.memset(d.c127[:], 127.0))
        self.v_ts(d.gflag[:], d.gflag[:], float(2.0 ** -40), -1.0, ALU.add, ALU.add)
        self.v_ts(d.gflag[:], d.gflag[:], float(2.0 ** 40), None, ALU.mult)
        self.v_ts(d.gm1f[:], d.gflag[:], 1.0, -1.0, ALU.subtract, ALU.mult)
        self.op("V", lambda e: e.reciprocal(d.c127[:], d.c127[:]), drain=True)
        self.v_ts(d.tiny.ap(0, 1) if False else d.t11a.ap(0, 1), d.c127[:], -127.0, 2.0,
                  ALU.mult, ALU.add)
        self.v_tt(d.c127[:], d.c127[:], d.t11a.ap(0, 1))

        self.wait("V", "sK", q_small)
        fold_tok = [0, 0]
        for i in range(8):
            half = i % 2
            src = bass.AP(d.d_wcat, i * 128 * 3072, [[3072, 128], [1, 3072]])
            if fold_tok[half]:
                self.wait("Q", "cV", fold_tok[half])
            tok = self.dma("Q", lambda e, src=src, half=half: e.dma_start(
                out=d.wstg.ap(half * 3072, 3072), in_=src), "sW%d" % half)
            self.wait("V", "sW%d" % half, tok)
            out_kv = bass.AP(d.wcatb, i * 3072, [[8 * 3072, 128], [192, 16], [1, 128]])
            in_kv = d.wstg.ap3(half * 3072, [(192, 16), (1, 128)])
            self.v_ts(out_kv, in_kv, d.snwc[:, i:i + 1], None, ALU.mult)
            out_q = bass.AP(d.wcatb, i * 3072 + 128, [[8 * 3072, 128], [192, 16], [1, 64]])
            in_q = d.wstg.ap3(half * 3072 + 128, [(192, 16), (1, 64)])
            fold_tok[half] = self.v_ts(out_q, in_q, d.rnwc[:, i:i + 1], None, ALU.mult, inc="cV")
        wab_src = bass.AP(d.d_wab, 0, [[48, 128], [128 * 48, 8], [1, 48]])
        self.wait("Q", "cV", fold_tok[0])
        self.wait("Q", "cV", fold_tok[1])
        tok = self.dma("Q", lambda e: e.dma_start(
            out=d.wstg.ap3(0, [(48, 8), (1, 48)]), in_=wab_src), "sW0")
        self.wait("V", "sW0", tok)
        snw_bc = bass.AP(d.snwc, 0, [[8, 128], [1, 8], [0, 48]])
        self.v_tt(bass.AP(d.wabb, 0, [[8 * 48, 128], [48, 8], [1, 48]]),
                  d.wstg.ap3(0, [(48, 8), (1, 48)]), snw_bc, ALU.mult, inc="cV")
        self.prep_done_q = q_small

    def p0(self, s):
        d = self
        self.ctx = 'p0s%d' % s
        xt = [d.xtile.ap(0, 1024), d.xtile.ap(1024, 1024)]
        self.tbar()
        self.wait("Q", "cV", self.n["cV"])
        self.wait("V", "sA0", self.n["sA0"])
        self.wait("V", "sA1", self.n["sA1"])
        if s == 0:
            self.wait("T", "sK", self.prep_done_q)
            self.wait("T", "sPW", self.prep_done_p)
        v_tok = [0] * T
        s_tok = [0] * T
        for t in range(T):
            if t >= 2:
                self.wait("Q", "cV", v_tok[t - 2])
                self.wait("Q", "cS", s_tok[t - 2])
            src = bass.AP(d.d_x, s * S * D + t * 128 * D, [[D, 128], [1, 1024]])
            sem = "sX%d" % (t % 2)
            ld = self.dma("Q", lambda e, src=src, t=t: e.dma_start(out=xt[t % 2], in_=src), sem)
            self.wait("V", sem, ld)
            v_tok[t] = self.op("V", lambda e, t=t: e.tensor_reduce(
                d.negsum.ap(t, 1), xt[t % 2], AX.X, ALU.add, negate=True), inc="cV",
                drain=(t == T - 1))
            self.wait("S", sem, ld)
            s_tok[t] = self.op("S", lambda e, t=t: e.activation(
                d.ch2.ap(), xt[t % 2], AF.Square, accum_out=d.ssqc.ap(t, 1)), inc="cS")
        self.wait("V", "cS", self.n["cS"])
        self.v_ts(d.mcol.ap(), d.negsum.ap(), -1.0 / 1024.0, None, ALU.mult)
        self.v_ts(d.nmcol.ap(), d.mcol.ap(), -1.0, None, ALU.mult)
        self.v_tt(d.tmpP0.ap(), d.mcol.ap(), d.mcol.ap())
        self.v_ts(d.varc.ap(), d.ssqc.ap(), 1.0 / 1024.0, None, ALU.mult)
        self.v_tt(d.varc.ap(), d.varc.ap(), d.tmpP0.ap(), ALU.subtract)
        tk = self.v_ts(d.tmpP0.ap(), d.varc.ap(), 1e-5, None, ALU.add, inc="cV")
        self.rsqrt_chain(d.rstd.ap(), d.tmpP0.ap(), d.htmpa.ap(), d.htmpb.ap(), arg_tok=tk)
        self.v_tt(d.htmpa.ap(), d.rstd.ap(), d.rstd.ap())
        self.v_tt(d.htmpa.ap(), d.htmpa.ap(), d.varc.ap())
        tk = self.v_ts(d.htmpa.ap(), d.htmpa.ap(), 1e-6, None, ALU.add, inc="cV")
        self.rsqrt_chain(d.rsc.ap(), d.htmpa.ap(), d.htmpb.ap(), d.tmpP0.ap(), arg_tok=tk)
        stats_tok = self.v_tt(d.rs2.ap(), d.rsc.ap(), d.rsc.ap(), inc="cV")
        if s == 0:
            self.dump("stats", d.scols[:], [128, 160])
        self.wait("Q", "cV", stats_tok)
        BAB = [d.B0, d.B1]
        xn_tok = [0] * T
        tr_tok = [0] * T
        ev_tok = [0] * T
        ab_tok = [0] * T
        sg_tok = [0] * T
        th_tok = [0] * T
        mm_tok = [0] * T
        for t in range(T):
            if t >= 2:
                self.wait("Q", "cV", xn_tok[t - 2])
            src = bass.AP(d.d_x, s * S * D + t * 128 * D, [[D, 128], [1, 1024]])
            sem = "sX%d" % (t % 2)
            ld = self.dma("Q", lambda e, src=src, t=t: e.dma_start(out=xt[t % 2], in_=src), sem)
            self.wait("V", sem, ld)
            xn_tok[t] = self.v_ts(d.xnb.ap(), xt[t % 2], d.mcol.ap(t, 1), d.rstd.ap(t, 1),
                                  ALU.subtract, ALU.mult, inc="cV")
            self.wait("T", "cV", xn_tok[t])
            halves = [([d.xnb.ap((4 * k + j) * 128, 128) for j in range(4)],
                       bass.AP(d.xnT, t * 128 + k * 4 * 2048,
                               [[8 * 2048, 128], [2048, 4], [1, 128]])) for k in range(2)]
            ev_tok[t] = self.transpose_halves(halves, pn=128)
            tr_tok[t] = ev_tok[t]
            self.wait("T", "cS", ev_tok[t])
            if t >= 2:
                self.wait("T", "cS", sg_tok[t - 2])
                self.wait("T", "cV", th_tok[t - 2])
            def abmm(e, t=t):
                ins = None
                for dd in range(8):
                    ins = e.matmul(BAB[t % 2][:, 0:48],
                                   bass.AP(d.xnT, dd * 2048 + t * 128, [[8 * 2048, 128], [1, 128]]),
                                   d.wabb[:, dd * 48:(dd + 1) * 48],
                                   start=(dd == 0), stop=(dd == 7))
                return ins
            ab_tok[t] = self.op("T", abmm, inc="cT")
            self.wait("V", "cT", ab_tok[t])
            if t >= 1:
                self.wait("V", "cS", sg_tok[t - 1])
            sg_in_tok = self.v_tt(d.sgin.ap(0, 48), BAB[t % 2][:, 0:48], d.abf[:],
                                  ALU.add, inc="cV", drain=False)
            self.wait("S", "cV", sg_in_tok)
            if t >= 1:
                self.wait("S", "cT", mm_tok[t - 1])
            sg_tok[t] = self.op("S", lambda e: e.activation(
                d.sgout.ap(0, 48), d.sgin.ap(0, 48), AF.Sigmoid), inc="cS")
            self.wait("V", "cS", sg_tok[t])
            th_tok[t] = self.v_ts(bass.AP(d.th2, t * 16, [[256, 128], [1, 16]]),
                                  d.sgout.ap(16, 16), 0.02, None, ALU.mult, inc="cV",
                                  drain=False)
            self.wait("T", "cS", sg_tok[t])
            mm_tok[t] = self.op("T", lambda e, t=t: e.matmul(
                d.B5[0:1, 0:32], d.onesf[:], d.sgout.ap3(0, [(32, 2), (1, 16)]),
                start=(t == 0), stop=(t == T - 1)), inc="cT")
        self.wait("V", "cT", mm_tok[T - 1])
        sc_tok = self.v_ts(bass.AP(d.ams, 0, [[32, 1], [1, 32]]), d.B5[0:1, 0:32],
                           1.0 / 2048.0, None, ALU.mult, inc="cV")
        self.wait("T", "cV", sc_tok)
        bc_tok = self.op("T", lambda e: e.matmul(
            d.B5[:, 32:64], d.ones1[:], bass.AP(d.ams, 0, [[32, 1], [1, 32]]),
            start=True, stop=True), inc="cT")
        self.wait("V", "cT", bc_tok)
        self.v_copy(d.amh[:], d.B5[:, 32:64], inc="cV")
        if s == 0:
            self.dump("xnT", d.xnT[:], [128, 16384])
            self.dump("th2", d.th2[:], [128, 256])
            self.dump("amh", d.amh[:], [128, 32])

    def p1A(self, s, h):
        self.p1A_start(s, h)
        self.p1A_tiles(s, h, 0, T)

    def p1A_start(self, s, h):
        if h == 0:
            self.tbar()
        self.a_ev = [0] * T
        if self.mat_tok:
            self.wait("S", "cV", self.mat_tok)

    def p1A_tiles(self, s, h, lo, hi):
        d = self
        self.ctx = 'p1Ah%d' % h
        BKV = [d.B0, d.B1]
        ev_tok = self.a_ev
        for t in range(lo, hi):
            if t == 0 and self.bank01_free:
                self.wtok("T", self.bank01_free)
            if t >= 2:
                self.wait("T", "cS", ev_tok[t - 2])
            def kvqmm(e, t=t):
                ins = None
                for dd in range(8):
                    ins = e.matmul(BKV[t % 2][:, 0:192],
                                   bass.AP(d.xnT, dd * 2048 + t * 128, [[8 * 2048, 128], [1, 128]]),
                                   d.wcatb[:, dd * 3072 + h * 192: dd * 3072 + (h + 1) * 192],
                                   start=(dd == 0), stop=(dd == 7))
                return ins
            mm = self.op("T", kvqmm, inc="cT")
            self.wait("S", "cT", mm)
            ev_tok[t] = self.s_copy(d.kvqf.ap(t * 192, 192), BKV[t % 2][:, 0:192])
        if hi == T:
            self.a_last = ("cS", ev_tok[T - 1])
        if s == 0 and h == 0 and hi == T:
            self.dump("kvqf", d.kvqf.ap(), [128, 3072])

    def p1B(self, s, h):
        d = self
        self.ctx = 'p1Bh%d' % h
        junk3 = d.ch2.ap3(0, [(64, 16), (1, 64)])
        hjunk3 = d.hnf.ap3(0, [(64, 16), (1, 64)])
        kAP0 = d.kvqf.ap3(0, [(192, 16), (1, 64)])
        qAP0 = d.kvqf.ap3(128, [(192, 16), (1, 64)])
        if self.dh_tok:
            self.wait("S", "cV", self.dh_tok)
        sk = self.op("S", lambda e: e.activation(junk3, kAP0, AF.Square), inc="cS")
        sq_ = self.op("S", lambda e: e.activation(hjunk3, qAP0, AF.Square), inc="cS")
        self.wait("V", "cS", sk)
        self.op("V", lambda e: e.tensor_reduce(d.kqsq_k.ap3(0, [(1, 16), (0, 1)]), junk3,
                                               AX.X, ALU.add), drain=False)
        self.wait("V", "cS", sq_)
        self.b_red_tok = self.op("V", lambda e: e.tensor_reduce(
            d.kqsq_q.ap3(0, [(1, 16), (0, 1)]), hjunk3, AX.X, ALU.add), inc="cV", drain=True)
        rs2bc = d.rs2.ap3(0, [(0, 2), (1, 16)])
        self.v_ts(d.kqsq.ap(), d.kqsq.ap(), 1.0 / 64.0, None, ALU.mult)
        self.v_tt(d.kqsq.ap(), d.kqsq.ap(), rs2bc)
        tk = self.v_ts(d.kqsq.ap(), d.kqsq.ap(), 1e-6, None, ALU.add, inc="cV")
        self.rsqrt_chain(d.rikq.ap(), d.kqsq.ap(), d.htmp32a.ap(), d.htmp32b.ap(), arg_tok=tk)
        self.v_tt(d.ckc.ap(), d.rikq_k.ap(), d.rsc.ap())
        self.v_tt(d.cqc.ap(), d.rikq_q.ap(), d.rsc.ap())

    def qCf_buf(self, h):
        return self.qCf if h % 2 == 0 else self.qCf2

    def p1C(self, s, h):
        d = self
        self.ctx = 'p1Ch%d' % h
        if self.g_last:
            self.wait("V", "cT", self.g_last)
        kAP = d.kvqf.ap3(0, [(192, 16), (1, 64)])
        vAP = d.kvqf.ap3(64, [(192, 16), (1, 64)])
        qAP = d.kvqf.ap3(128, [(192, 16), (1, 64)])
        qb = self.qCf_buf(h)
        kCf3 = d.keysCf.ap3(0, [(64, 16), (1, 64)])
        qCf3 = qb.ap3(0, [(64, 16), (1, 64)])
        self.v_tt(kCf3, kAP, d.ckc.ap3(0, [(1, 16), (0, 64)]))
        self.v_tt(d.keysCb.ap3(0, [(64, 16), (1, 64)]), kCf3,
                  bass.AP(d.knwf, 0, [[64, 128], [0, 16], [1, 64]]))
        self.v_tt(d.valsCb.ap3(0, [(64, 16), (1, 64)]), vAP, d.rsc.ap3(0, [(1, 16), (0, 64)]))
        self.v_tt(qCf3, qAP, d.cqc.ap3(0, [(1, 16), (0, 64)]))
        self.v_tt(qCf3, qCf3, bass.AP(d.qnwf, 0, [[64, 128], [0, 16], [1, 64]]))
        self.mat_tok = self.v_copy(d.qCb.ap(), qb.ap(), inc="cV")
        if s == 0 and h == 0:
            self.dump("keysCf", d.keysCf.ap(), [128, 1024])
            self.dump("qCf", qb.ap(), [128, 1024])
            self.dump("valsCb", d.valsCb.ap(), [128, 1024])

    def transpose_halves(self, halves, pn=64):
        d = self
        tok = None
        for (aps, dst_ap) in halves:
            hf = self.bt_half
            self.bt_half ^= 1
            if self.bt_free_h[hf]:
                self.wait("T", "cS", self.bt_free_h[hf])
            buf = d.BT0 if hf == 0 else d.BT1
            def trs(e, aps=aps, pn=pn, buf=buf):
                ins = None
                for i, in_ap in enumerate(aps):
                    ins = e.transpose(buf[0:pn, i * 128:(i + 1) * 128], in_ap, d.identb[:])
                return ins
            tk = self.op("T", trs, inc="cT")
            self.wait("S", "cT", tk)
            tok = self.s_copy(dst_ap, buf[0:pn, 0:512])
            self.bt_free_h[hf] = tok
        self.bt_free = tok
        return tok

    def p1D(self, s, h):
        d = self
        self.ctx = 'p1Dh%d' % h
        self.wait("T", "cV", self.mat_tok)
        for (srcb, dstb) in ((d.keysCb, d.keysT), (d.qCb, d.qT)):
            for g in range(2):
                halves = [([srcb.ap((g * 8 + 4 * k + j) * 64, 64) for j in range(4)],
                           dstb.ap(g * 1024 + k * 512, 512, pn=64)) for k in range(2)]
                self.transpose_halves(halves, pn=64)
            if dstb is d.keysT:
                self.keysT_ready = self.bt_free
        if s == 0 and h == 0:
            self.dump("keysT", d.keysT.ap(), [128, 2048])

    def p1EF(self, s, h):
        d = self
        self.ctx = 'p1EFh%d' % h
        B01 = [d.B0, d.B1]
        NP_ = T // 2
        f_toks = [0] * 8
        self.wait("T", "cS", self.keysT_ready)
        self.wtok("T", self.a_last)
        for u in range(8):
            bank = [d.B0, d.B1][u % 2]
            if u >= 2:
                self.wait("T", "cS", f_toks[u - 2])
            def p0T(e, u=u, bank=bank):
                return e.matmul(bank[:, 0:512],
                                d.w0b[0:64, (u % 2) * 128:(u % 2) * 128 + 128],
                                d.keysT.ap((u // 2) * 512, 512, pn=64), start=True, stop=True)
            mm = self.op("T", p0T, inc="cT")
            self.wait("S", "cT", mm)
            f_toks[u] = self.op("S", lambda e, u=u, bank=bank: e.activation(
                d.act1Tb.ap((u % 2) * 2048 + (u // 2) * 512, 512), bank[:, 0:512], AF.Gelu),
                inc="cS")
        self.act1T_ready = f_toks[7]
        self.f_toks = f_toks
        if s == 0 and h == 0:
            self.dump("act1Tb", d.act1Tb.ap(), [128, 4096])
        act_tok = [None] * NP_
        self.wait("T", "cS", f_toks[7])
        for p in range(NP_):
            if p >= 2:
                self.wait("T", act_tok[p - 2][0], act_tok[p - 2][1])
            def e_mm(e, p=p):
                e.matmul(B01[p % 2][:, 0:256], d.keysT.ap(2 * p * 128, 128, pn=64), d.w0b[:],
                         start=True, stop=True)
                return e.matmul(B01[p % 2][:, 256:512], d.keysT.ap((2 * p + 1) * 128, 128, pn=64),
                                d.w0b[:], start=True, stop=True)
            mm = self.op("T", e_mm, inc="cT")
            self.wait("S", "cT", mm)
            self.op("S", lambda e, p=p: e.activation(d.act1b.ap(p * 512, 512),
                                                     B01[p % 2][:, 0:512], AF.Gelu))
            act_tok[p] = self.emit_gg(B01[p % 2][:, 0:512], d.ggb.ap(p * 512, 512), mm)
        self.bank01_free = act_tok[NP_ - 1]
        if s == 0 and h == 0:
            self.dump("act1b", d.act1b.ap(), [128, 4096])
            self.dump("ggb", d.ggb.ap(), [128, 4096])

    def p1G(self, s, h):
        d = self
        self.ctx = 'p1Gh%d' % h
        B23 = [d.B2, d.B3]
        NG = T // 4
        p1_tok = [0] * NG
        if self.dh_tok:
            self.wait("S", "cV", self.dh_tok)
        if self.l_last:
            self.wtok("T", self.l_last)
        for g in range(NG):
            self.wait("T", "cS", self.f_toks[2 * g + 1])
            if g >= 2:
                self.wait("T", "cS", p1_tok[g - 2])
            def p1mm(e, g=g):
                ins = None
                for j in range(4):
                    t = 4 * g + j
                    e.matmul(B23[g % 2][:, j * 64:(j + 1) * 64], d.act1Tb.ap(t * 128, 128),
                             d.w1b[:, 0:64], start=True, stop=False)
                    ins = e.matmul(B23[g % 2][:, j * 64:(j + 1) * 64],
                                   d.act1Tb.ap(2048 + t * 128, 128),
                                   d.w1b[:, 64:128], start=False, stop=True)
                return ins
            mm = self.op("T", p1mm, inc="cT")
            self.wait("S", "cT", mm)
            p1_tok[g] = self.s_copy(d.pre1f.ap(g * 256, 256), B23[g % 2][:, 0:256])
        self.g_toks = ("cS", p1_tok[NG - 1])

    def p1H(self, s, h):
        d = self
        self.ctx = 'p1Hh%d' % h
        junk3 = d.ch2.ap3(0, [(64, 16), (1, 64)])
        p13 = d.pre1f.ap3(0, [(64, 16), (1, 64)])
        self.wtok("V", self.g_toks)
        self.v_tt(junk3, p13, p13)
        self.h_red_tok = self.op("V", lambda e: e.tensor_reduce(
            d.p1sq.ap3(0, [(1, 16), (0, 1)]), junk3, AX.X, ALU.add), inc="cV", drain=True)
        tk = self.v_ts(d.htmpa.ap(), d.p1sq.ap(), 1.0 / 64.0, 1e-8, ALU.mult, ALU.add, inc="cV")
        self.rsqrt_chain(d.rinv.ap(), d.htmpa.ap(), d.htmpb.ap(), d.p1sq.ap(), arg_tok=tk)
        if s == 0 and h == 0:
            self.dump("pre1f", d.pre1f.ap(), [128, 1024])
            self.dump("hcols", d.hcols[:], [128, 256])

    def p1I(self, s, h):
        d = self
        self.ctx = 'p1Ih%d' % h
        rinv_bc = d.rinv.ap3(0, [(1, 16), (0, 64)])
        pre1_3 = d.pre1f.ap3(0, [(64, 16), (1, 64)])
        hn3 = d.hnf.ap3(0, [(64, 16), (1, 64)])
        ch1_3 = d.ch1.ap3(0, [(64, 16), (1, 64)])
        ch2_3 = d.ch2.ap3(0, [(64, 16), (1, 64)])
        th2_2d = bass.AP(d.th2, h, [[256, 128], [16, 16]])
        self.v_tt(hn3, pre1_3, rinv_bc)
        self.v_tt(d.ch1.ap(), d.hnf.ap(), d.keysCb.ap(), ALU.add)
        self.v_tt(d.ch1.ap(), d.ch1.ap(), d.valsCb.ap(), ALU.subtract)
        self.v_tt(d.ch2.ap(), d.ch1.ap(), d.hnf.ap())
        self.op("V", lambda e: e.tensor_reduce(d.dyv.ap3(0, [(1, 16), (0, 1)]),
                                               ch2_3, AX.X, ALU.add), drain=True)
        self.v_tt(d.htmpa.ap(), th2_2d, d.rinv.ap())
        self.v_ts(d.dyv.ap(), d.dyv.ap(), 1.0 / 64.0, None, ALU.mult)
        self.v_tt(d.htmpb.ap(), d.dyv.ap(), d.htmpa.ap())
        self.v_tt(ch2_3, hn3, d.htmpb.ap3(0, [(1, 16), (0, 64)]))
        self.v_tt(ch1_3, ch1_3, d.htmpa.ap3(0, [(1, 16), (0, 64)]))
        self.dh_tok = self.v_tt(d.dhb.ap(), d.ch1.ap(), d.ch2.ap(), ALU.subtract, inc="cV")
        if s == 0 and h == 0:
            self.dump("dhb", d.dhb.ap(), [128, 1024])

    def p1J(self, s, h, a_cb=None):
        d = self
        self.ctx = 'p1Jh%d' % h
        B23 = [d.B2, d.B3]
        self.wait("T", "cV", self.dh_tok)
        for g in range(2):
            halves = [([d.dhb.ap((g * 8 + 4 * k + j) * 64, 64) for j in range(4)],
                       d.dhT.ap(g * 1024 + k * 512, 512, pn=64)) for k in range(2)]
            self.transpose_halves(halves, pn=64)
        self.wait("T", "cS", self.bt_free)
        if self.bank456_free:
            self.wait("T", "cV", self.bank456_free[0])
            self.wait("T", "cS", self.bank456_free[1])
        NP_ = T // 2
        mult_tok = [0] * NP_
        for p in range(NP_):
            if p >= 2:
                self.wait("T", "cV", mult_tok[p - 2])
            def dh0mm(e, p=p):
                e.matmul(B23[p % 2][:, 0:256], d.dhT.ap(2 * p * 128, 128, pn=64), d.w1tb[:],
                         start=True, stop=True)
                return e.matmul(B23[p % 2][:, 256:512], d.dhT.ap((2 * p + 1) * 128, 128, pn=64),
                                d.w1tb[:], start=True, stop=True)
            dh0 = self.op("T", dh0mm, inc="cT")
            def g1mm(e, p=p):
                ins = None
                for j in range(2):
                    t = 2 * p + j
                    e.matmul(d.B4[:, 0:64], d.act1b.ap(t * 256, 128), d.dhb.ap(t * 64, 64),
                             start=(t == 0), stop=(t == T - 1))
                    ins = e.matmul(d.B4[:, 64:128], d.act1b.ap(t * 256 + 128, 128),
                                   d.dhb.ap(t * 64, 64), start=False, stop=(t == T - 1))
                return ins
            self.op("T", g1mm)
            self.wait("V", "cT", dh0)
            mult_tok[p] = self.v_tt(d.dh0full.ap(p * 512, 512), B23[p % 2][:, 0:512],
                                    d.ggb.ap(p * 512, 512), inc="cV")
            if a_cb is not None:
                a_cb(p)
                self.ctx = 'p1Jh%d' % h
        self.wait("T", "cV", mult_tok[NP_ - 1])
        def g0mm(e):
            ins = None
            for t in range(T):
                ins = e.matmul(d.B5[0:64, 0:256], d.keysCb.ap(t * 64, 64),
                               d.dh0full.ap((t // 2) * 512 + (t % 2) * 256, 256),
                               start=(t == 0), stop=(t == T - 1))
            return ins
        g_last = self.op("T", g0mm, inc="cT")
        self.j_mult_last = (mult_tok[NP_ - 1], mult_tok[NP_ - 2])
        self.g_last = g_last

    def p1K(self, s, h):
        d = self
        self.ctx = 'p1Kh%d' % h
        self.wait("S", "cT", self.g_last)
        if self.b_red_tok:
            self.wait("S", "cV", self.b_red_tok)
        self.op("S", lambda e: e.activation(d.ch2.ap(0, 256, pn=64), d.B5[0:64, 0:256], AF.Square,
                                            accum_out=d.t11a.ap(0, 1, pn=64)))
        self.op("S", lambda e: e.activation(d.ch2.ap(0, 64), d.B4[:, 0:64], AF.Square,
                                            accum_out=d.t11b.ap(0, 1)))
        ssq_tok = self.op("S", lambda e: e.activation(d.ch2.ap(0, 64), d.B4[:, 64:128], AF.Square,
                                                      accum_out=d.t11c.ap(0, 1)), inc="cS")
        self.wait("T", "cS", ssq_tok)
        def ssqmm(e):
            e.matmul(d.B5[0:1, 480:481], d.onesf[0:64, 0:1], d.t11a.ap(0, 1, pn=64),
                     start=True, stop=False)
            e.matmul(d.B5[0:1, 480:481], d.onesf[:], d.t11b.ap(0, 1), start=False, stop=False)
            return e.matmul(d.B5[0:1, 480:481], d.onesf[:], d.t11c.ap(0, 1),
                            start=False, stop=True)
        mm = self.op("T", ssqmm, inc="cT")
        self.wait("S", "cT", mm)
        sq = self.op("S", lambda e: e.activation(d.t11a.ap(0, 1, pn=1), d.B5[0:1, 480:481],
                                                 AF.Sqrt), inc="cS")
        self.wait("V", "cS", sq)
        self.v_ts(d.t11a.ap(0, 1, pn=1), d.t11a.ap(0, 1, pn=1), 1e-6, None, ALU.add)
        self.op("V", lambda e: e.reciprocal(d.t11b.ap(0, 1, pn=1), d.t11a.ap(0, 1, pn=1)),
                drain=True)
        cf = self.v_ts(d.coef11.ap(0, 1, pn=1), d.t11b.ap(0, 1, pn=1), 10.0, 1.0,
                       ALU.mult, ALU.min, inc="cV")
        self.wait("T", "cV", cf)
        bc = self.op("T", lambda e: e.matmul(d.B5[:, 481:482], d.ones1[:],
                                             d.coef11.ap(0, 1, pn=1), start=True, stop=True),
                     inc="cT")
        self.wait("V", "cT", bc)
        self.v_copy(d.coefb.ap(0, 1), d.B5[:, 481:482])
        self.v_ts(d.negcoef.ap(0, 1), d.coefb.ap(0, 1), -1.0, None, ALU.mult)
        self.v_ts(d.onea.ap(0, 1), d.amh[:, h:h + 1], 1.0, -1.0, ALU.subtract, ALU.mult)
        self.v_ts(d.wtmp.ap(0, 256, pn=64), d.w0f[:], d.onea.ap(0, 1, pn=64), None, ALU.mult,
                  drain=False)
        self.v_stt(d.w0pb[:], d.B5[0:64, 0:256], d.negcoef.ap(0, 1, pn=64),
                   d.wtmp.ap(0, 256, pn=64), ALU.mult, ALU.add)
        self.v_ts(d.wtmp.ap(0, 128), d.w1f[:], d.onea.ap(0, 1), None, ALU.mult, drain=False)
        self.v_stt(d.w1pb[:, 0:64], d.B4[:, 0:64], d.negcoef.ap(0, 1),
                   d.wtmp.ap(0, 64), ALU.mult, ALU.add)
        wp_tok = self.v_stt(d.w1pb[:, 64:128], d.B4[:, 64:128], d.negcoef.ap(0, 1),
                            d.wtmp.ap(64, 64), ALU.mult, ALU.add, inc="cV")
        self.wp_tok = wp_tok
        self.ssq_tok = ssq_tok
        self.bank456_free = (wp_tok, ssq_tok)
        if s == 0 and h == 0:
            self.dump("w0pb", d.w0pb[:], [64, 256])
            self.dump("w1pb", d.w1pb[:], [128, 128])
            self.dump("coefb", d.coefb.ap(0, 1), [128, 1])

    def p1L(self, s, h):
        d = self
        self.ctx = 'p1Lh%d' % h
        B23 = [d.B2, d.B3]
        self.wait("T", "cV", self.wp_tok)
        self.wait("T", "cS", self.ssq_tok)
        f_toks = [0] * 8
        for u in range(8):
            bank = [d.B4, d.B5][u % 2]
            if u >= 2:
                self.wait("T", "cS", f_toks[u - 2])
            def p0pT(e, u=u, bank=bank):
                return e.matmul(bank[:, 0:512],
                                d.w0pb[0:64, (u % 2) * 128:(u % 2) * 128 + 128],
                                d.qT.ap((u // 2) * 512, 512, pn=64), start=True, stop=True)
            mm = self.op("T", p0pT, inc="cT")
            self.wait("S", "cT", mm)
            f_toks[u] = self.op("S", lambda e, u=u, bank=bank: e.activation(
                d.act1Tb.ap((u % 2) * 2048 + (u // 2) * 512, 512), bank[:, 0:512], AF.Gelu),
                inc="cS")
        NG = T // 4
        p1_tok = [0] * NG
        if self.n_at:
            self.wait("S", "cV", self.n_at)
        self.wait("T", "cS", f_toks[7])
        self.wait("T", "cV", self.j_mult_last[0])
        self.wait("T", "cV", self.j_mult_last[1])
        for g in range(NG):
            if g >= 2:
                self.wait("T", "cS", p1_tok[g - 2])
            def p1pmm(e, g=g):
                ins = None
                for j in range(4):
                    t = 4 * g + j
                    e.matmul(B23[g % 2][:, j * 64:(j + 1) * 64], d.act1Tb.ap(t * 128, 128),
                             d.w1pb[:, 0:64], start=True, stop=False)
                    ins = e.matmul(B23[g % 2][:, j * 64:(j + 1) * 64],
                                   d.act1Tb.ap(2048 + t * 128, 128),
                                   d.w1pb[:, 64:128], start=False, stop=True)
                return ins
            mm = self.op("T", p1pmm, inc="cT")
            self.wait("S", "cT", mm)
            p1_tok[g] = self.s_copy(d.pre1pf.ap(g * 256, 256), B23[g % 2][:, 0:256])
        self.l_last = ("cS", p1_tok[NG - 1])

    def p1M(self, s, h):
        d = self
        self.ctx = 'p1Mh%d' % h
        junk3 = d.ch2.ap3(0, [(64, 16), (1, 64)])
        pp3 = d.pre1pf.ap3(0, [(64, 16), (1, 64)])
        if self.dh_tok:
            self.wait("S", "cV", self.dh_tok)
        self.wtok("S", self.l_last)
        msq = self.op("S", lambda e: e.activation(junk3, pp3, AF.Square), inc="cS")
        self.wait("V", "cS", msq)
        self.op("V", lambda e: e.tensor_reduce(d.p1psq.ap3(0, [(1, 16), (0, 1)]), junk3,
                                               AX.X, ALU.add), drain=True)
        tk = self.v_ts(d.htmpc.ap(), d.p1psq.ap(), 1.0 / 64.0, 1e-8, ALU.mult, ALU.add, inc="cV")
        self.rsqrt_chain(d.rinvp.ap(), d.htmpc.ap(), d.htmpd.ap(), d.p1psq.ap(), arg_tok=tk)

    def p1N(self, s, h):
        d = self
        self.ctx = 'p1Nh%d' % h
        if self.attn_tok:
            self.wait("V", "sAT", self.attn_tok)
        qb = self.qCf_buf(h)
        self.v_tt(d.attnf.ap3(0, [(64, 16), (1, 64)]), d.pre1pf.ap3(0, [(64, 16), (1, 64)]),
                  d.rinvp.ap3(0, [(1, 16), (0, 64)]))
        at = self.n_at = self.v_tt(d.attnf.ap(), d.attnf.ap(), qb.ap(), ALU.add, inc="cV")
        if s == 0 and h == 0:
            self.dump("attnf", d.attnf.ap(), [128, 1024])
        self.wait("Q", "cV", at)
        dst = bass.AP(self.d_out, s * S * D + h * 64, [[D, 128], [128 * D, 16], [1, 64]])
        self.attn_tok = self.dma("Q", lambda e, dst=dst: e.dma_start(
            out=dst, in_=d.attnf.ap3(0, [(64, 16), (1, 64)])), "sAT")

    def emit_gg(self, psum_ap, out_ap, mm_tok):
        d = self
        if not SIM_FP64:
            tok = self.op("S", lambda e: e.activation(out_ap, psum_ap, AF.Derivative_Gelu),
                          inc="cS")
            return ("cS", tok)
        W = 256
        gel = d.ch1.ap(0, W)
        e1 = d.hnf.ap(0, W)
        e2 = d.pre1f.ap(0, W)
        m = d.outt.ap(0, W)
        tok = self.op("S", lambda e: e.activation(gel, psum_ap, AF.Gelu), inc="cS")
        self.wait("V", "cS", tok)
        self.v_ts(m, psum_ap, 0.0, None, ALU.abs_max)
        self.v_ts(m, m, 1e-30, None, ALU.is_ge)
        self.v_ts(e1, m, 1.0, None, ALU.subtract)
        self.v_tt(e2, psum_ap, e1, ALU.subtract)
        self.v_tt(gel, gel, e2, ALU.divide)
        self.v_tt(gel, gel, m)
        self.v_ts(e1, e1, -0.5, None, ALU.mult)
        self.v_tt(gel, gel, e1, ALU.add)
        self.v_tt(e1, psum_ap, psum_ap)
        tok = self.v_copy(e1, e1, inc="cV")
        self.wait("S", "cV", tok)
        tok = self.op("S", lambda e: e.activation(e2, e1, AF.Exp, scale=-0.5), inc="cS")
        self.wait("V", "cS", tok)
        self.v_tt(e2, e2, psum_ap)
        chi = float(np.float32(0.3989422804014327))
        clo = float(np.float64(0.3989422804014327) - np.float64(np.float32(0.3989422804014327)))
        self.v_ts(e1, e2, chi, None, ALU.mult)
        self.v_tt(gel, gel, e1, ALU.add)
        self.v_ts(e1, e2, clo, None, ALU.mult)
        tok = self.v_tt(out_ap, gel, e1, ALU.add, inc="cV")
        return ("cV", tok)

    def p2(self, s):
        d = self
        self.ctx = 'p2s%d' % s
        xt = [d.xtile.ap(0, 1024), d.xtile.ap(1024, 1024)]
        self.tbar()
        self.wait("Q", "cV", self.n["cV"])
        attn_done = self.n["sAT"]
        xc_tok = [0] * T
        xb_tok = [0] * T
        ev_tok = [0] * T
        ffn_tok = [0] * T
        acc_tok = [0] * T
        cur_tok = [0] * T
        cur_used = [0] * T
        for t in range(T):
            if t >= 2:
                self.wait("Q", "cV", xc_tok[t - 2])
                self.wait("Q", "cV", xb_tok[t - 2])
            src = bass.AP(d.d_x, s * S * D + t * 128 * D, [[D, 128], [1, 1024]])
            sem = "sX%d" % (t % 2)
            ld = self.dma("Q", lambda e, src=src, t=t: e.dma_start(out=xt[t % 2], in_=src), sem)
            self.wait("V", sem, ld)
            xc_tok[t] = self.v_ts(d.ch1.ap(), xt[t % 2], d.mcol.ap(t, 1), None,
                                  ALU.subtract, inc="cV")
            a = d.htmpa.ap(0, 1)
            b = d.htmpb.ap(0, 1)
            r = d.htmpa.ap(1, 1)
            cmb = d.htmpb.ap(1, 1)
            self.op("V", lambda e: e.reduce_max(a, d.ch1.ap(), AX.X,
                                                apply_absolute_value=True), drain=True)
            self.v_tt(b, a, d.rstd.ap(t, 1))
            self.v_ts(b, b, 1e-6, None, ALU.max)
            self.op("V", lambda e: e.reciprocal(r, b), drain=True)
            self.v_tt(d.xscl.ap(t, 1), b, d.c127[:])
            self.v_ts(r, r, 127.0, None, ALU.mult)
            self.v_tt(cmb, r, d.rstd.ap(t, 1))
            self.v_ts(d.ch2.ap(), d.ch1.ap(), cmb, None, ALU.mult)
            self.v_ts(d.hnf.ap(), d.ch2.ap(), M32, -M32, ALU.add, ALU.add)
            self.v_ts(d.pre1f.ap(), d.ch2.ap(), M52, -M52, ALU.add, ALU.add)
            self.v_ts(d.pre1f.ap(), d.pre1f.ap(), d.gflag[:], None, ALU.mult)
            q_tok = self.v_stt(d.xqbt.ap(), d.hnf.ap(), d.gm1f[:], d.pre1f.ap(),
                               ALU.mult, ALU.add, inc="cV")
            self.wait("T", "cV", q_tok)
            halves = [([d.xqbt.ap((4 * k + j) * 128, 128) for j in range(4)],
                       d.xqTt.ap((t % 2) * 1024 + k * 512, 512)) for k in range(2)]
            ev_tok[t] = self.transpose_halves(halves, pn=128)
            self.wait("T", "cS", ev_tok[t])
            if t >= 1:
                self.wait("T", "cV", ffn_tok[t - 1])
            def ffnmm(e, t=t):
                ins = None
                for j in range(8):
                    lhsT = d.xqTt.ap((t % 2) * 1024 + j * 128, 128)
                    e.matmul(d.B0[:, 0:512], lhsT, d.wqtb[:, j * 1024: j * 1024 + 512],
                             start=(j == 0), stop=(j == 7))
                    ins = e.matmul(d.B1[:, 0:512], lhsT,
                                   d.wqtb[:, j * 1024 + 512: (j + 1) * 1024],
                                   start=(j == 0), stop=(j == 7))
                return ins
            mm = self.op("T", ffnmm, inc="cT")
            if t == 0:
                self.wait("Q", "sAT", attn_done)
            if t >= 2:
                self.wait("Q", "cV", cur_used[t - 2])
            srcc = bass.AP(self.d_out, s * S * D + t * 128 * D, [[D, 128], [1, 1024]])
            cur_tok[t] = self.dma("Q", lambda e, srcc=srcc, t=t: e.dma_start(
                out=d.curb.ap((t % 2) * 1024, 1024), in_=srcc), "sC%d" % (t % 2))
            self.wait("V", "cT", mm)
            if t >= 2:
                self.wait("V", "sA%d" % (t % 2), acc_tok[t - 2])
            xb_tok[t] = self.v_tt(d.ch2.ap(), xt[t % 2], d.biasf[:], ALU.add, inc="cV")
            xscol = d.xscl.ap(t, 1)
            self.v_tt(d.hnf.ap(0, 512), d.B0[:, 0:512], d.wscf[:, 0:512])
            self.v_stt(d.outt.ap((t % 2) * 1024, 512), d.hnf.ap(0, 512), xscol,
                       d.ch2.ap(0, 512), ALU.mult, ALU.add)
            self.v_tt(d.hnf.ap(512, 512), d.B1[:, 0:512], d.wscf[:, 512:1024])
            self.v_stt(d.outt.ap((t % 2) * 1024 + 512, 512), d.hnf.ap(512, 512),
                       xscol, d.ch2.ap(512, 512), ALU.mult, ALU.add)
            self.wait("V", "sC%d" % (t % 2), cur_tok[t])
            ffn_tok[t] = cur_used[t] = self.v_tt(
                d.outt.ap((t % 2) * 1024, 1024), d.outt.ap((t % 2) * 1024, 1024),
                d.curb.ap((t % 2) * 1024, 1024), ALU.add, inc="cV")
            self.wait("P", "cV", ffn_tok[t])
            dstp = bass.AP(self.d_out, s * S * D + t * 128 * D, [[D, 128], [1, 1024]])
            acc_tok[t] = self.dma("P", lambda e, dstp=dstp, t=t: e.dma_start(
                out=dstp, in_=d.outt.ap((t % 2) * 1024, 1024)), "sA%d" % (t % 2))
        if s == 0:
            self.dump("xqbt15", d.xqbt.ap(), [128, 1024])
            self.dump("outt", d.outt.ap(), [128, 2048])


def _build_and_run(shards, host):
    mk = MK()
    nc = mk.build()
    in_maps = []
    for xc in shards:
        m = dict(host)
        m["x"] = xc
        in_maps.append(m)
    res = run_bass_kernel_spmd(nc, in_maps, core_ids=list(range(len(shards))))
    return np.concatenate([r["out"] for r in res.results], axis=0)


def kernel(L0_attn_x, L0_attn_W_K, L0_attn_W_V, L0_attn_W_Q, L0_attn_mem_W0, L0_attn_mem_W1,
           L0_attn_k_norm_w, L0_attn_q_norm_w, L0_attn_store_norm_w, L0_attn_retrieve_norm_w,
           L0_attn_alpha_w, L0_attn_alpha_b, L0_attn_theta_w, L0_attn_theta_b,
           L0_attn_eta_w, L0_attn_eta_b, L0_ffn_weight_q, L0_ffn_weight_scale, L0_ffn_bias):
    f32 = lambda a: np.ascontiguousarray(np.asarray(a), dtype=np.float32)
    x = f32(L0_attn_x)
    wk, wv, wq = f32(L0_attn_W_K), f32(L0_attn_W_V), f32(L0_attn_W_Q)
    wcat = np.empty((D, H, 192), np.float32)
    wcat[:, :, 0:64] = wk.reshape(D, H, 64)
    wcat[:, :, 64:128] = wv.reshape(D, H, 64)
    wcat[:, :, 128:192] = wq.reshape(D, H, 64)
    wcat = np.ascontiguousarray(wcat.reshape(D, H * 192))
    wab = np.ascontiguousarray(np.concatenate(
        [f32(L0_attn_alpha_w), f32(L0_attn_theta_w), f32(L0_attn_eta_w)], axis=1))
    ab = np.ascontiguousarray(np.concatenate(
        [f32(L0_attn_alpha_b), f32(L0_attn_theta_b), f32(L0_attn_eta_b)]))
    w1 = f32(L0_attn_mem_W1)
    host = {
        "wcat": wcat, "wab": wab, "ab": ab,
        "w0": f32(L0_attn_mem_W0), "w1": w1,
        "w1t": np.ascontiguousarray(w1.T),
        "knw": f32(L0_attn_k_norm_w), "qnw": f32(L0_attn_q_norm_w),
        "snw": f32(L0_attn_store_norm_w), "rnw": f32(L0_attn_retrieve_norm_w),
        "wqt8": np.ascontiguousarray(np.asarray(L0_ffn_weight_q).T),
        "wsc": f32(L0_ffn_weight_scale), "fb": f32(L0_ffn_bias),
        "ident": np.eye(128, dtype=np.float32).astype(ml_dtypes.bfloat16),
    }
    shards = [np.ascontiguousarray(x[c * BPC:(c + 1) * BPC]) for c in range(NCORE)]
    out = _build_and_run(shards, host)
    return out
